# version 06 (final)
# baseline (speedup 1.0000x reference)
import numpy as np
import ml_dtypes
from contextlib import ExitStack
import concourse.bass as bass
import concourse.mybir as mybir
from concourse.bass_utils import run_bass_kernel_spmd

F32 = mybir.dt.float32
BF16 = mybir.dt.bfloat16
I32 = mybir.dt.int32
AF = mybir.ActivationFunctionType
OP = mybir.AluOpType
AX = mybir.AxisListType

NF4 = [-1.0, -0.6961928009986877, -0.5250730514526367, -0.39491748809814453,
       -0.28444138169288635, -0.18477343022823334, -0.09105003625154495, 0.0,
       0.07958029955625534, 0.16093020141124725, 0.24611230194568634,
       0.33791524171829224, 0.44070982933044434, 0.5626170039176941,
       0.6989939212799072, 1.0]

NC_ = 8
D = 4096
JW = 512
HID = 1408
SWEEPS = [(0, 3), (3, 6), (6, 9), (9, 11)]
SCALE = 128.0 ** -0.5


def build_nc(T=4096, NB=2):
    NTOK = NB * T
    SHARD = NTOK // NC_
    CH_T = min(512, SHARD)
    N_CH = NTOK // CH_T
    TT_B = T // 128
    NP1 = NB * TT_B
    NQC = T // 512
    NHB = NB * 4
    NLT = SHARD // 128
    N_SC = NTOK // 512
    NTC = NB * (T // 512)

    nc = bass.Bass("TRN2", target_bir_lowering=False)
    x_sh = nc.dram_tensor("x_sh", [SHARD, D], F32, kind="ExternalInput").ap()
    xT_own = nc.dram_tensor("xT_own", [512, NTOK], F32, kind="ExternalInput").ap()
    wqT = nc.dram_tensor("wqT", [D, JW], F32, kind="ExternalInput").ap()
    wkT = nc.dram_tensor("wkT", [D, JW], F32, kind="ExternalInput").ap()
    wvT = nc.dram_tensor("wvT", [D, JW], F32, kind="ExternalInput").ap()
    woT = nc.dram_tensor("woT", [JW, D], F32, kind="ExternalInput").ap()
    c1T = nc.dram_tensor("c1T", [D, HID], I32, kind="ExternalInput").ap()
    c3T = nc.dram_tensor("c3T", [D, HID], I32, kind="ExternalInput").ap()
    c2T = nc.dram_tensor("c2T", [HID, D], I32, kind="ExternalInput").ap()
    a1T = nc.dram_tensor("a1T", [D, HID], F32, kind="ExternalInput").ap()
    a3T = nc.dram_tensor("a3T", [D, HID], F32, kind="ExternalInput").ap()
    a2T = nc.dram_tensor("a2T", [HID, D], F32, kind="ExternalInput").ap()
    cs_in = nc.dram_tensor("cs_in", [T, 2], F32, kind="ExternalInput").ap()
    dmask = nc.dram_tensor("dmask", [4, 128, 512], F32, kind="ExternalInput").ap()
    ident_in = nc.dram_tensor("ident_in", [128, 128], F32, kind="ExternalInput").ap()
    outp = nc.dram_tensor("outp", [512, NTOK], F32, kind="ExternalOutput").ap()

    cc_xn = nc.dram_tensor("cc_xn", [D, SHARD], BF16)
    xn_ag = nc.dram_tensor("xn_ag", [NC_ * D, SHARD], BF16)
    q_fm = nc.dram_tensor("q_fm", [NHB, 128, T], BF16)
    k_fm = nc.dram_tensor("k_fm", [NHB, 128, T], BF16)
    v_tm = nc.dram_tensor("v_tm", [NHB, T, 128], BF16)
    attn_d = nc.dram_tensor("attn_d", [NHB, 128, T], BF16)
    w1Ts = nc.dram_tensor("w1Ts", [D, HID], BF16)
    w3Ts = nc.dram_tensor("w3Ts", [D, HID], BF16)
    w2Ts = nc.dram_tensor("w2Ts", [HID, D], BF16)
    s_part = nc.dram_tensor("s_part", [NTC, D, 512], F32)
    y_part = nc.dram_tensor("y_part", [N_CH, D, CH_T], F32)
    s_red = nc.dram_tensor("s_red", [NTC, 512, 512], F32)
    y_red = nc.dram_tensor("y_red", [N_CH, 512, CH_T], F32)

    xn_v = xn_ag.ap().rearrange("(r d) t -> r d t", r=NC_)

    dq_jobs = []
    for ti in range(32):
        dq_jobs.append((c1T, a1T, w1Ts, ti * 128, 0, HID))
    for ti in range(32):
        dq_jobs.append((c3T, a3T, w3Ts, ti * 128, 0, HID))
    for ti in range(11):
        for cj, w in ((0, 1408), (1408, 1408), (2816, 1280)):
            dq_jobs.append((c2T, a2T, w2Ts, ti * 128, cj, w))
    n_dq = len(dq_jobs)
    v_order = list(range(n_dq))
    nv = n_dq

    with ExitStack() as top:
        E = top.enter_context
        ident = E(nc.sbuf_tensor("ident", [128, 128], BF16))
        maskt = E(nc.sbuf_tensor("maskt", [128, 4 * 512], F32))
        cst = E(nc.sbuf_tensor("cst", [128, 2 * TT_B], F32))
        sx = E(nc.sbuf_tensor("sx", [128, 1], F32))
        sx2 = E(nc.sbuf_tensor("sx2", [128, 1], F32))
        mean = E(nc.sbuf_tensor("mean", [128, 1], F32))
        m2 = E(nc.sbuf_tensor("m2", [128, 1], F32))
        msq = E(nc.sbuf_tensor("msq", [128, 1], F32))
        var = E(nc.sbuf_tensor("var", [128, 1], F32))
        std = E(nc.sbuf_tensor("std", [128, 1], F32))
        rstd = E(nc.sbuf_tensor("rstd", [128, 1], F32))
        nmx_c = E(nc.sbuf_tensor("nmx_c", [128, 4], F32))
        sums_c = E(nc.sbuf_tensor("sums_c", [128, 4], F32))
        rsum_c = E(nc.sbuf_tensor("rsum_c", [128, 4], F32))
        psA = [E(nc.psum_tensor(f"psA{i}", [128, 512], F32)) for i in range(6)]
        psT = [E(nc.psum_tensor(f"psT{i}", [128, 512], BF16)) for i in range(2)]
        psB = psA[5]

        sems = ["cl", "wl", "lx0", "lx1", "lns", "g2", "pes", "pt", "agw", "cc",
                "dql0", "dql1", "dqc2", "dqw", "dqst0", "dqst1", "dqg", "rpg",
                "xl10", "xl11", "pe1", "rp", "ptr", "aev", "aevs", "spv", "spq",
                "al0", "al1", "pe3", "sev", "sfm", "rcp", "rnm", "pvd",
                "ast", "wol", "a4l0", "a4l1", "pe4", "sst0", "sst1", "ccs",
                "wfl", "xl20", "xl21", "peh", "hev", "hgt", "pey", "yev", "yst", "ccy",
                "fl", "fc", "fo"]
        S = {n: E(nc.semaphore(n)) for n in sems}
        block = E(nc.Block())

        @block.gpsimd
        def _(g):
            g.dma_start(out=ident[:, :], in_=ident_in[:, :]).then_inc(S["wl"], 16)
            g.wait_ge(S["agw"], 16 * 32)
            g.collective_compute("AllGather", OP.bypass,
                                 replica_groups=[list(range(NC_))],
                                 ins=[cc_xn.ap().opt()],
                                 outs=[xn_ag.ap().opt()]).then_inc(S["cc"], 1)

        @block.sync
        def _(sy):
            for i in range(4):
                sy.dma_start(out=maskt[:, i * 512:(i + 1) * 512],
                             in_=dmask[i, :, :]).then_inc(S["cl"], 16)
            for i in range(TT_B):
                sy.dma_start(out=cst[:, 2 * i:2 * i + 2],
                             in_=cs_in[i * 128:(i + 1) * 128, :]).then_inc(S["cl"], 16)

        with ExitStack() as dsc:
            ED = dsc.enter_context
            dqc = [ED(nc.sbuf_tensor(f"dqc{i}", [128, 1408], I32)) for i in range(2)]
            dqb = ED(nc.sbuf_tensor("dqb", [128, 1408], BF16))
            dqsl = ED(nc.sbuf_tensor("dqsl", [128, 6 * 1408], BF16))
            dqm = [ED(nc.sbuf_tensor(f"dqm{i}", [128, 1408], F32)) for i in range(2)]
            dqo = [ED(nc.sbuf_tensor(f"dqo{i}", [128, 1408], BF16)) for i in range(2)]

            dq_li = [0]
            dq_ci = [0]
            dq_si = [0]
            dq_gc = [0]
            SELV = [(float(vv), NF4[vv]) for vv in range(16) if NF4[vv] != 0.0]

            def emit_dq_load(eng):
                i = dq_li[0]
                if i >= nv:
                    return
                dq_li[0] += 1
                src_c, src_a, dst, r0, c0, w = dq_jobs[v_order[i]]
                if i >= 2:
                    eng.wait_ge(S["dqc2"], i - 1)
                    eng.wait_ge(S["dqw"], i - 1)
                eng.dma_start(out=dqc[i % 2][:, 0:w],
                              in_=src_c[r0:r0 + 128, c0:c0 + w]).then_inc(S[f"dql{i % 2}"], 16)
                eng.dma_start(out=dqm[i % 2][:, 0:w],
                              in_=src_a[r0:r0 + 128, c0:c0 + w]).then_inc(S[f"dql{i % 2}"], 16)

            def emit_dq_comp(v):
                i = dq_ci[0]
                if i >= nv:
                    return
                dq_ci[0] += 1
                _, _, dst, r0, c0, w = dq_jobs[v_order[i]]
                v.wait_ge(S[f"dql{i % 2}"], 32 * (i // 2 + 1))
                if i >= 2:
                    v.wait_ge(S[f"dqst{i % 2}"], 16 * (i // 2))
                v.tensor_copy(dqb[:, 0:w], dqc[i % 2][:, 0:w]).then_inc(S["dqc2"], 1)
                v.wait_ge(S["dqc2"], i + 1)

                def sl(j):
                    return dqsl[:, j * 1408: j * 1408 + w]
                pend = list(SELV)
                filled = []
                free = [0, 1, 2, 3, 4, 5]
                while pend or len(filled) > 1:
                    nxt = []
                    adds = []
                    sels = []
                    while len(filled) >= 2:
                        a = filled.pop(0)
                        b = filled.pop(0)
                        adds.append((a, b))
                        nxt.append(a)
                        free.append(b)
                    while free and pend:
                        s_ = free.pop(0)
                        sels.append((s_, pend.pop(0)))
                        nxt.append(s_)
                    for a, b in adds:
                        v.tensor_add(sl(a), sl(a), sl(b)).then_inc(S["dqg"], 1)
                        dq_gc[0] += 1
                    for s_, (vv, tv) in sels:
                        v.tensor_scalar(sl(s_), dqb[:, 0:w], vv, tv,
                                        OP.is_equal, OP.mult).then_inc(S["dqg"], 1)
                        dq_gc[0] += 1
                    v.wait_ge(S["dqg"], dq_gc[0])
                    filled = nxt + filled
                v.tensor_mul(dqo[i % 2][:, 0:w], sl(filled[0]),
                             dqm[i % 2][:, 0:w]).then_inc(S["dqw"], 1)

            def emit_dq_store(eng):
                i = dq_si[0]
                if i >= nv:
                    return
                dq_si[0] += 1
                _, _, dst, r0, c0, w = dq_jobs[v_order[i]]
                eng.wait_ge(S["dqw"], i + 1)
                eng.dma_start(out=dst.ap()[r0:r0 + 128, c0:c0 + w],
                              in_=dqo[i % 2][:, 0:w]).then_inc(S[f"dqst{i % 2}"], 16)

            with ExitStack() as sc0:
                E0 = sc0.enter_context
                xt = [E0(nc.sbuf_tensor(f"xt{i}", [128, D], F32)) for i in range(2)]
                xnt = E0(nc.sbuf_tensor("xnt", [128, D], BF16))
                sqscr = E0(nc.sbuf_tensor("sqscr", [128, D], BF16))
                stage = E0(nc.sbuf_tensor("stage", [128, 32 * SHARD], BF16))

                @block.sync
                def _(sy):
                    for i in range(NLT):
                        if i >= 2:
                            sy.wait_ge(S["pt"], (i - 1) * 32)
                        sy.dma_start(out=xt[i % 2][:, :],
                                     in_=x_sh[i * 128:(i + 1) * 128, :]).then_inc(S[f"lx{i % 2}"], 16)
                        emit_dq_load(sy)

                @block.scalar
                def _(s):
                    for i in range(NLT):
                        s.wait_ge(S[f"lx{i % 2}"], 16 * (i // 2 + 1))
                        if i >= 1:
                            s.wait_ge(S["g2"], 7 * (i - 1) + 3)
                        s.activation(sqscr[:, :], xt[i % 2][:, :], AF.Square,
                                     accum_out=sx2[:, :]).then_inc(S["lns"], 1)
                        s.wait_ge(S["g2"], 7 * i + 5)
                        s.activation(std[:, :], var[:, :], AF.Sqrt).then_inc(S["lns"], 1)
                        emit_dq_store(s)

                @block.vector
                def _(v):
                    for i in range(NLT):
                        b7 = 7 * i
                        v.wait_ge(S[f"lx{i % 2}"], 16 * (i // 2 + 1))
                        v.reduce_sum(sx[:, :], xt[i % 2][:, :], AX.X).then_inc(S["g2"], 1)
                        v.wait_ge(S["lns"], 2 * i + 1)
                        v.wait_ge(S["g2"], b7 + 1)
                        v.tensor_scalar_mul(mean[:, :], sx[:, :], 1.0 / D).then_inc(S["g2"], 1)
                        v.wait_ge(S["g2"], b7 + 2)
                        v.tensor_scalar_mul(m2[:, :], sx2[:, :], 1.0 / D).then_inc(S["g2"], 1)
                        v.wait_ge(S["g2"], b7 + 3)
                        v.tensor_scalar_mul(msq[:, :], mean[:, :], mean[:, :]).then_inc(S["g2"], 1)
                        v.wait_ge(S["g2"], b7 + 4)
                        v.tensor_scalar(var[:, :], m2[:, :], msq[:, :], 1e-5,
                                        OP.subtract, OP.add).then_inc(S["g2"], 1)
                        v.wait_ge(S["lns"], 2 * i + 2)
                        v.reciprocal(rstd[:, :], std[:, :]).then_inc(S["g2"], 1)
                        v.wait_ge(S["g2"], b7 + 6)
                        if i >= 1:
                            v.wait_ge(S["pes"], 32 * i)
                        v.tensor_scalar(xnt[:, :], xt[i % 2][:, :], mean[:, :], rstd[:, :],
                                        OP.subtract, OP.mult).then_inc(S["g2"], 1)
                        for r in range(8):
                            v.wait_ge(S["pes"], i * 32 + (r + 1) * 4)
                            for k in range(4):
                                j = r * 4 + k
                                ci_ = v.tensor_copy(
                                    stage[:, j * SHARD + i * 128: j * SHARD + (i + 1) * 128],
                                    psT[r % 2][:, k * 128:(k + 1) * 128])
                            ci_.then_inc(S["pt"], 4)
                        emit_dq_comp(v)

                @block.tensor
                def _(t):
                    t.wait_ge(S["wl"], 16)
                    for i in range(NLT):
                        t.wait_ge(S["g2"], 7 * i + 7)
                        for r in range(8):
                            if i * 32 + r * 4 >= 8:
                                t.wait_ge(S["pt"], i * 32 + r * 4 - 4)
                            for k in range(4):
                                j = r * 4 + k
                                t.transpose(psT[r % 2][:, k * 128:(k + 1) * 128],
                                            xnt[:, j * 128:(j + 1) * 128],
                                            ident[:, :]).then_inc(S["pes"], 1)

                @block.scalar
                def _(s):
                    s.wait_ge(S["pt"], NLT * 32)
                    for j in range(32):
                        s.dma_start(out=cc_xn.ap()[j * 128:(j + 1) * 128, :],
                                    in_=stage[:, j * SHARD:(j + 1) * SHARD]).then_inc(S["agw"], 16)
                    for _ in range(dq_ci[0] - dq_si[0]):
                        emit_dq_store(s)

            with ExitStack() as sc1:
                E1 = sc1.enter_context
                wq_sb = E1(nc.sbuf_tensor("wq_sb", [128, 32 * JW], BF16))
                wk_sb = E1(nc.sbuf_tensor("wk_sb", [128, 32 * JW], BF16))
                wv_sb = E1(nc.sbuf_tensor("wv_sb", [128, 32 * JW], BF16))
                p1xn = [E1(nc.sbuf_tensor(f"p1x{i}", [128, 32 * 128], BF16)) for i in range(2)]
                ropet = E1(nc.sbuf_tensor("ropet", [128, 512], F32))
                qr_sb = E1(nc.sbuf_tensor("qr_sb", [128, JW], BF16))
                kr_sb = E1(nc.sbuf_tensor("kr_sb", [128, JW], BF16))
                vr_sb = E1(nc.sbuf_tensor("vr_sb", [128, JW], BF16))
                tstg = [E1(nc.sbuf_tensor(f"tstg{i}", [128, 512], BF16)) for i in range(2)]

                @block.gpsimd
                def _(g):
                    g.wait_ge(S["agw"], 16 * 32)
                    for j in range(32):
                        g.dma_start(out=wq_sb[:, j * JW:(j + 1) * JW],
                                    in_=wqT[j * 128:(j + 1) * 128, :]).then_inc(S["wl"], 16)
                        g.dma_start(out=wk_sb[:, j * JW:(j + 1) * JW],
                                    in_=wkT[j * 128:(j + 1) * 128, :]).then_inc(S["wl"], 16)
                        g.dma_start(out=wv_sb[:, j * JW:(j + 1) * JW],
                                    in_=wvT[j * 128:(j + 1) * 128, :]).then_inc(S["wl"], 16)

                @block.sync
                def _(sy):
                    for idx in range(NP1):
                        b, ti = divmod(idx, TT_B)
                        r_sh, t0 = divmod(b * T + ti * 128, SHARD)
                        if idx == 0:
                            sy.wait_ge(S["cc"], 1)
                        if idx >= 2:
                            sy.wait_ge(S["pe1"], 96 * (idx - 1))
                        for dj in range(32):
                            sy.dma_start(out=p1xn[idx % 2][:, dj * 128:(dj + 1) * 128],
                                         in_=xn_v[r_sh, dj * 128:(dj + 1) * 128,
                                                  t0:t0 + 128]).then_inc(S[f"xl1{idx % 2}"], 16)
                        emit_dq_load(sy)

                @block.tensor
                def _(t):
                    t.wait_ge(S["wl"], 16 * 97)
                    for idx in range(NP1):
                        t.wait_ge(S[f"xl1{idx % 2}"], 16 * 32 * (idx // 2 + 1))
                        if idx >= 2:
                            t.wait_ge(S["rp"], 3 * (idx - 1))
                        for pi, w_sb in enumerate((wq_sb, wk_sb, wv_sb)):
                            pb = psA[(idx % 2) * 3 + pi]
                            for dj in range(32):
                                t.matmul(pb[:, :], p1xn[idx % 2][:, dj * 128:(dj + 1) * 128],
                                         w_sb[:, dj * JW:(dj + 1) * JW],
                                         start=(dj == 0), stop=(dj == 31)).then_inc(S["pe1"], 1)
                        if idx >= 1:
                            pidx = idx - 1
                            t.wait_ge(S["rp"], 3 * (pidx + 1))
                            if pidx >= 1:
                                t.wait_ge(S["aev"], 8 * pidx)
                            for gi, src in enumerate((qr_sb, kr_sb)):
                                for h in range(4):
                                    t.transpose(psT[gi][:, h * 128:(h + 1) * 128],
                                                src[:, h * 128:(h + 1) * 128],
                                                ident[:, :]).then_inc(S["ptr"], 1)
                    t.wait_ge(S["rp"], 3 * NP1)
                    t.wait_ge(S["aev"], 8 * (NP1 - 1))
                    for gi, src in enumerate((qr_sb, kr_sb)):
                        for h in range(4):
                            t.transpose(psT[gi][:, h * 128:(h + 1) * 128],
                                        src[:, h * 128:(h + 1) * 128],
                                        ident[:, :]).then_inc(S["ptr"], 1)

                @block.vector
                def _(v):
                    v.wait_ge(S["cl"], 16 * (4 + TT_B))
                    for idx in range(NP1):
                        b, ti = divmod(idx, TT_B)
                        co = cst[:, 2 * ti:2 * ti + 1]
                        si = cst[:, 2 * ti + 1:2 * ti + 2]
                        v.wait_ge(S["pe1"], 96 * idx + 32)
                        if idx >= 1:
                            v.wait_ge(S["ptr"], 8 * idx)
                            v.wait_ge(S["spv"], 16 * 4 * idx)
                        pq = psA[(idx % 2) * 3 + 0]
                        pk = psA[(idx % 2) * 3 + 1]
                        pv = psA[(idx % 2) * 3 + 2]
                        for gi, (ps_, dst) in enumerate(((pq, qr_sb), (pk, kr_sb))):
                            if gi == 1:
                                v.wait_ge(S["pe1"], 96 * idx + 64)
                            base_r = 4 * (2 * idx + gi)
                            v.tensor_scalar_mul(ropet[:, 0:256], ps_[:, 1:JW:2], si).then_inc(S["rpg"], 1)
                            v.wait_ge(S["rpg"], base_r + 1)
                            v.scalar_tensor_tensor(dst[:, 0:JW:2], ps_[:, 0:JW:2], co,
                                                   ropet[:, 0:256], OP.mult, OP.subtract).then_inc(S["rpg"], 1)
                            v.tensor_scalar_mul(ropet[:, 256:512], ps_[:, 0:JW:2], si).then_inc(S["rpg"], 1)
                            v.wait_ge(S["rpg"], base_r + 3)
                            v.scalar_tensor_tensor(dst[:, 1:JW:2], ps_[:, 1:JW:2], co,
                                                   ropet[:, 256:512], OP.mult, OP.add).then_inc(S["rpg"], 1)
                        v.wait_ge(S["pe1"], 96 * (idx + 1))
                        v.tensor_copy(vr_sb[:, :], pv[:, :]).then_inc(S["rp"], 3)
                        emit_dq_comp(v)

                @block.scalar
                def _(s):
                    for idx in range(NP1):
                        b, ti = divmod(idx, TT_B)
                        s.wait_ge(S["rp"], 3 * (idx + 1))
                        for h in range(4):
                            s.dma_start(out=v_tm.ap()[b * 4 + h, ti * 128:(ti + 1) * 128, :],
                                        in_=vr_sb[:, h * 128:(h + 1) * 128]).then_inc(S["spv"], 16)
                        pidx = idx - 1
                        if pidx >= 0:
                            pb_, pti = divmod(pidx, TT_B)
                            s.wait_ge(S["ptr"], 8 * (pidx + 1))
                            if pidx >= 1:
                                s.wait_ge(S["spq"], 16 * 8 * pidx)
                            for gi, dstd in enumerate((q_fm, k_fm)):
                                s.copy(tstg[gi][:, :], psT[gi][:, :]).then_inc(S["aev"], 4)
                                s.wait_ge(S["aev"], 8 * pidx + 4 * (gi + 1))
                                for h in range(4):
                                    s.dma_start(
                                        out=dstd.ap()[pb_ * 4 + h, :, pti * 128:(pti + 1) * 128],
                                        in_=tstg[gi][:, h * 128:(h + 1) * 128]).then_inc(S["spq"], 16)
                        emit_dq_store(s)
                    pidx = NP1 - 1
                    pb_, pti = divmod(pidx, TT_B)
                    s.wait_ge(S["ptr"], 8 * NP1)
                    s.wait_ge(S["spq"], 16 * 8 * pidx)
                    for gi, dstd in enumerate((q_fm, k_fm)):
                        s.copy(tstg[gi][:, :], psT[gi][:, :]).then_inc(S["aev"], 4)
                        s.wait_ge(S["aev"], 8 * pidx + 4 * (gi + 1))
                        for h in range(4):
                            s.dma_start(out=dstd.ap()[pb_ * 4 + h, :, pti * 128:(pti + 1) * 128],
                                        in_=tstg[gi][:, h * 128:(h + 1) * 128]).then_inc(S["spq"], 16)

            NPTR1 = 8 * NP1
            NAEV1 = 8 * NP1

            pe3c = 0
            sevc = 0
            sfmc = 0
            rcpc = 0
            rnmc = 0
            ptrc = NPTR1
            aevvc = NAEV1
            aevsc = 0
            pvdc = 0
            alc = 0
            sched = []
            for hb in range(NHB):
                alc += 16 * 3 * (T // 512)
                for tqc in range(NQC):
                    nkc = tqc + 1
                    tiles = []
                    for tqt in range(4):
                        rec = dict(al=alc, pe3_0=pe3c, sev_0=sevc, sfm_0=sfmc,
                                   rcp_0=rcpc, rnm_0=rnmc)
                        pe3c += nkc
                        sevc += nkc + 1
                        sfmc += 1
                        rcpc += 1
                        rnmc += 1
                        rec.update(pe3_1=pe3c, sev_1=sevc, sfm_1=sfmc,
                                   rcp_1=rcpc, rnm_1=rnmc)
                        tiles.append(rec)
                    kts = []
                    for kt in range(4 * nkc):
                        krec = dict(ptr_0=ptrc, pe3_0=pe3c)
                        if kt % 2 == 0:
                            krec["cps"] = "aev"
                            krec["cp_0"] = aevvc
                            aevvc += 4
                            krec["cp_1"] = aevvc
                        else:
                            krec["cps"] = "aevs"
                            krec["cp_0"] = aevsc
                            aevsc += 4
                            krec["cp_1"] = aevsc
                        ptrc += 4
                        pe3c += 1
                        krec.update(ptr_1=ptrc, pe3_1=pe3c)
                        kts.append(krec)
                    pvdc += 1
                    sched.append(dict(hb=hb, tqc=tqc, nkc=nkc, tiles=tiles, kts=kts,
                                      pvd=pvdc))
            NPE3 = pe3c
            NPVD = 16 * NHB * NQC

            with ExitStack() as sc3:
                E3 = sc3.enter_context
                qf = E3(nc.sbuf_tensor("qf", [128, T], BF16))
                kf = E3(nc.sbuf_tensor("kf", [128, T], BF16))
                vf = E3(nc.sbuf_tensor("vf", [128, T], BF16))
                ssb = E3(nc.sbuf_tensor("ssb", [128, 2 * T], F32))
                probs = E3(nc.sbuf_tensor("probs", [128, 4 * T], BF16))
                probsT = [E3(nc.sbuf_tensor(f"pT{i}", [128, 512], BF16)) for i in range(2)]
                astg = E3(nc.sbuf_tensor("astg", [128, 512], BF16))

                @block.sync
                def _(sy):
                    for hb in range(NHB):
                        if hb == 0:
                            sy.wait_ge(S["spq"], 16 * 8 * NP1)
                            sy.wait_ge(S["spv"], 16 * 4 * NP1)
                        else:
                            sy.wait_ge(S["pvd"], 16 * hb * NQC)
                        for j in range(T // 512):
                            sy.dma_start(out=qf[:, j * 512:(j + 1) * 512],
                                         in_=q_fm.ap()[hb, :, j * 512:(j + 1) * 512]).then_inc(S[f"al{hb % 2}"], 16)
                            sy.dma_start(out=kf[:, j * 512:(j + 1) * 512],
                                         in_=k_fm.ap()[hb, :, j * 512:(j + 1) * 512]).then_inc(S[f"al{hb % 2}"], 16)
                            sy.dma_start(
                                out=vf[:, j * 512:(j + 1) * 512].rearrange(
                                    "p (a h) -> p a h", h=128),
                                in_=v_tm.ap()[hb, j * 512:(j + 1) * 512, :]
                                .rearrange("(a p) h -> p a h", p=128)).then_inc(S[f"al{hb % 2}"], 16)
                        for _ in range(NQC):
                            emit_dq_load(sy)

                @block.tensor
                def _(t):
                    first = True
                    for ch in sched:
                        nkc = ch["nkc"]
                        for tqt, rec in enumerate(ch["tiles"]):
                            hb_ = ch["hb"]
                            per_hb = 16 * 3 * (T // 512)
                            t.wait_ge(S[f"al{hb_ % 2}"], per_hb * (hb_ // 2 + 1))
                            if first:
                                t.wait_ge(S["rp"], 3 * NP1)
                                first = False
                            if rec["sev_0"] >= 1:
                                t.wait_ge(S["sev"], rec["sev_0"])
                            qi = ch["tqc"] * 4 + tqt
                            for kc in range(nkc):
                                if kc >= 2:
                                    t.wait_ge(S["sev"], rec["sev_0"] + kc - 1)
                                t.matmul(psA[kc % 2][:, :], qf[:, qi * 128:(qi + 1) * 128],
                                         kf[:, kc * 512:(kc + 1) * 512],
                                         start=True, stop=True).then_inc(S["pe3"], 1)
                        t.wait_ge(S["rnm"], ch["tiles"][3]["rnm_1"])
                        for kt, krec in enumerate(ch["kts"]):
                            if kt >= 2:
                                t.wait_ge(S[krec["cps"]], krec["cp_0"])
                            if kt == 0:
                                t.wait_ge(S["pvd"], 16 * (ch["pvd"] - 1))
                            for tqt in range(4):
                                t.transpose(psT[kt % 2][:, tqt * 128:(tqt + 1) * 128],
                                            probs[:, tqt * T + kt * 128: tqt * T + (kt + 1) * 128],
                                            ident[:, :]).then_inc(S["ptr"], 1)
                            t.wait_ge(S[krec["cps"]], krec["cp_1"])
                            t.matmul(psB[:, :], vf[:, kt * 128:(kt + 1) * 128],
                                     probsT[kt % 2][:, :],
                                     start=(kt == 0),
                                     stop=(kt == 4 * nkc - 1)).then_inc(S["pe3"], 1)

                @block.vector
                def _(v):
                    v.wait_ge(S["cl"], 16 * (4 + TT_B))
                    for ci, ch in enumerate(sched):
                        nkc = ch["nkc"]
                        for tqt, rec in enumerate(ch["tiles"]):
                            g = 4 * ci + tqt
                            hoff = (g % 2) * T
                            kc = nkc - 1
                            v.wait_ge(S["pe3"], rec["pe3_0"] + kc + 1)
                            if g >= 2:
                                v.wait_ge(S["sfm"], g - 1)
                            v.scalar_tensor_tensor(ssb[:, hoff + kc * 512: hoff + (kc + 1) * 512],
                                                   psA[kc % 2][:, :], SCALE,
                                                   maskt[:, tqt * 512:(tqt + 1) * 512],
                                                   OP.mult, OP.add).then_inc(S["sev"], 1)
                            if ci >= 1:
                                v.wait_ge(S["sfm"], 4 * (ci - 1) + tqt + 1)
                            v.wait_ge(S["sev"], rec["sev_0"] + nkc)
                            v.reduce_max(nmx_c[:, tqt:tqt + 1],
                                         ssb[:, hoff: hoff + nkc * 512], AX.X,
                                         negate=True).then_inc(S["sev"], 1)
                        for tqt, rec in enumerate(ch["tiles"]):
                            v.wait_ge(S["sfm"], rec["sfm_1"])
                            v.reciprocal(rsum_c[:, tqt:tqt + 1],
                                         sums_c[:, tqt:tqt + 1]).then_inc(S["rcp"], 1)
                        for tqt, rec in enumerate(ch["tiles"]):
                            v.wait_ge(S["rcp"], rec["rcp_1"])
                            v.tensor_scalar_mul(probs[:, tqt * T: tqt * T + nkc * 512],
                                                probs[:, tqt * T: tqt * T + nkc * 512],
                                                rsum_c[:, tqt:tqt + 1]).then_inc(S["rnm"], 1)
                        for kt, krec in enumerate(ch["kts"]):
                            if kt % 2 != 0:
                                continue
                            v.wait_ge(S["ptr"], krec["ptr_1"])
                            if kt >= 2:
                                v.wait_ge(S["pe3"], krec["pe3_0"] - 1)
                            v.tensor_copy(probsT[0][:, :],
                                          psT[0][:, :]).then_inc(S["aev"], 4)
                        emit_dq_comp(v)

                @block.scalar
                def _(s):
                    for ci, ch in enumerate(sched):
                        nkc = ch["nkc"]
                        if ci >= 1:
                            s.wait_ge(S["ptr"], sched[ci - 1]["kts"][-1]["ptr_1"])
                            s.wait_ge(S["rcp"], 4 * ci)
                        for tqt, rec in enumerate(ch["tiles"]):
                            g = 4 * ci + tqt
                            hoff = (g % 2) * T
                            for kc in range(nkc - 1):
                                s.wait_ge(S["pe3"], rec["pe3_0"] + kc + 1)
                                s.activation(ssb[:, hoff + kc * 512: hoff + (kc + 1) * 512],
                                             psA[kc % 2][:, :], AF.Copy,
                                             scale=SCALE).then_inc(S["sev"], 1)
                            s.wait_ge(S["sev"], rec["sev_1"])
                            s.activation(probs[:, tqt * T: tqt * T + nkc * 512],
                                         ssb[:, hoff: hoff + nkc * 512],
                                         AF.Exp, bias=nmx_c[:, tqt:tqt + 1],
                                         accum_out=sums_c[:, tqt:tqt + 1]).then_inc(S["sfm"], 1)
                        for kt, krec in enumerate(ch["kts"]):
                            if kt % 2 != 1:
                                continue
                            s.wait_ge(S["ptr"], krec["ptr_1"])
                            if kt >= 2:
                                s.wait_ge(S["pe3"], krec["pe3_0"] - 1)
                            s.copy(probsT[1][:, :], psT[1][:, :]).then_inc(S["aevs"], 4)
                        s.wait_ge(S["pe3"], ch["kts"][-1]["pe3_1"])
                        s.wait_ge(S["pvd"], 16 * (ch["pvd"] - 1))
                        s.copy(astg[:, :], psB[:, :]).then_inc(S["ast"], 1)
                        s.wait_ge(S["ast"], ch["pvd"])
                        s.dma_start(out=attn_d.ap()[ch["hb"], :,
                                                    ch["tqc"] * 512:(ch["tqc"] + 1) * 512],
                                    in_=astg[:, :]).then_inc(S["pvd"], 16)
                        emit_dq_store(s)

            with ExitStack() as sc4:
                E4 = sc4.enter_context
                wo_sb = E4(nc.sbuf_tensor("wo_sb", [128, 4 * D], BF16))
                a4 = E4(nc.sbuf_tensor("a4", [128, 8 * 512], BF16))
                sstg = [E4(nc.sbuf_tensor(f"sstg{i}", [128, 512], F32)) for i in range(2)]

                @block.gpsimd
                def _(g):
                    g.wait_ge(S["pvd"], NPVD)
                    for j in range(4):
                        g.dma_start(out=wo_sb[:, j * D:(j + 1) * D],
                                    in_=woT[j * 128:(j + 1) * 128, :]).then_inc(S["wol"], 16)

                @block.sync
                def _(sy):
                    for tcg in range(NTC):
                        b, tc = divmod(tcg, T // 512)
                        if tcg == 0:
                            sy.wait_ge(S["pvd"], NPVD)
                        if tcg >= 2:
                            sy.wait_ge(S["pe4"], (tcg - 1) * 32)
                        for h in range(4):
                            sy.dma_start(out=a4[:, ((tcg % 2) * 4 + h) * 512:((tcg % 2) * 4 + h + 1) * 512],
                                         in_=attn_d.ap()[b * 4 + h, :,
                                                         tc * 512:(tc + 1) * 512]).then_inc(S[f"a4l{tcg % 2}"], 16)
                        emit_dq_load(sy)

                @block.tensor
                def _(t):
                    t.wait_ge(S["wol"], 16 * 4)
                    for tcg in range(NTC):
                        t.wait_ge(S[f"a4l{tcg % 2}"], 16 * 4 * (tcg // 2 + 1))
                        for oj in range(32):
                            idx = tcg * 32 + oj
                            if idx >= 2:
                                t.wait_ge(S["pe4"], idx - 1)
                            for h in range(4):
                                t.matmul(psA[idx % 2][:, :],
                                         wo_sb[:, h * D + oj * 128: h * D + (oj + 1) * 128],
                                         a4[:, ((tcg % 2) * 4 + h) * 512:((tcg % 2) * 4 + h + 1) * 512],
                                         start=(h == 0), stop=(h == 3)).then_inc(S["pe3"], 1)

                @block.scalar
                def _(s):
                    for tcg in range(NTC):
                        for oj in range(32):
                            idx = tcg * 32 + oj
                            s.wait_ge(S["pe3"], NPE3 + 4 * (idx + 1))
                            if idx >= 2:
                                s.wait_ge(S[f"sst{idx % 2}"], 16 * (idx // 2))
                            s.copy(sstg[idx % 2][:, :], psA[idx % 2][:, :]).then_inc(S["pe4"], 1)
                            s.wait_ge(S["pe4"], idx + 1)
                            s.dma_start(out=s_part.ap()[tcg, oj * 128:(oj + 1) * 128, :],
                                        in_=sstg[idx % 2][:, :]).then_inc(S[f"sst{idx % 2}"], 16)
                        emit_dq_store(s)

                @block.vector
                def _(v):
                    for tcg in range(NTC):
                        emit_dq_comp(v)
                        emit_dq_comp(v)

                @block.gpsimd
                def _(g):
                    for k in range(NTC):
                        n_st = 32 * (k + 1)
                        g.wait_ge(S["sst0"], 16 * ((n_st + 1) // 2))
                        g.wait_ge(S["sst1"], 16 * (n_st // 2))
                        g.collective_compute("ReduceScatter", OP.add,
                                             replica_groups=[list(range(NC_))],
                                             ins=[s_part.ap()[k, :, :].opt()],
                                             outs=[s_red.ap()[k, :, :].opt()]).then_inc(S["ccs"], 1)

            while dq_ci[0] < nv or dq_li[0] < nv:
                @block.sync
                def _(sy):
                    emit_dq_load(sy)
                    if dq_li[0] < min(nv, dq_ci[0] + 2):
                        emit_dq_load(sy)

                @block.vector
                def _(v):
                    emit_dq_comp(v)

                @block.scalar
                def _(s):
                    emit_dq_store(s)
            while dq_si[0] < nv:
                @block.scalar
                def _(s):
                    emit_dq_store(s)

            NDQS0 = 16 * ((nv + 1) // 2)
            NDQS1 = 16 * (nv // 2)

        SW2 = [(0, 2), (2, 4), (4, 6), (6, 8), (8, 10), (10, 11)]
        NSW = len(SW2)
        with ExitStack() as sc2:
            E2 = sc2.enter_context
            w13 = [E2(nc.sbuf_tensor(f"w13_{i}", [128, 32 * 512], BF16)) for i in range(2)]
            w2s = E2(nc.sbuf_tensor("w2s", [128, 2 * D], BF16))
            p2xn = E2(nc.sbuf_tensor("p2xn", [128, 32 * CH_T], BF16))
            hsb = [E2(nc.sbuf_tensor(f"hsb{i}", [128, CH_T], BF16)) for i in range(2)]
            htb = E2(nc.sbuf_tensor("htb", [128, 2 * CH_T], BF16))
            y_sb = E2(nc.sbuf_tensor("y_sb", [128, 32 * CH_T], F32))

            p2s = []
            wflc = 0
            pehc = 0
            hevc = 0
            hgtc = 0
            peyc = 0
            yevc = 0
            w2lc = 0
            for c in range(N_CH):
                for sw, (hj0, hj1) in enumerate(SW2):
                    nhj = hj1 - hj0
                    rec = dict(c=c, sw=sw, hj0=hj0, nhj=nhj,
                               wfl_0=wflc, peh_0=pehc, hev_0=hevc, hgt_0=hgtc,
                               pey_0=peyc, yev_0=yevc, w2l_0=w2lc)
                    wflc += 16 * 2 * 32
                    pehc += nhj
                    hevc += nhj
                    hgtc += nhj
                    peyc += 32
                    yevc += 32
                    w2lc += 16 * nhj
                    rec.update(wfl_1=wflc, peh_1=pehc, hev_1=hevc, hgt_1=hgtc,
                               pey_1=peyc, yev_1=yevc, w2l_1=w2lc)
                    p2s.append(rec)

            @block.sync
            def _(sy):
                for si, rec in enumerate(p2s):
                    hj0, nhj = rec["hj0"], rec["nhj"]
                    if si == 0:
                        sy.wait_ge(S["dqst0"], NDQS0)
                        sy.wait_ge(S["dqst1"], NDQS1)
                        sy.wait_ge(S["sst0"], 16 * ((32 * NTC + 1) // 2))
                        sy.wait_ge(S["sst1"], 16 * ((32 * NTC) // 2))
                    if si >= 2:
                        sy.wait_ge(S["peh"], p2s[si - 2]["peh_1"])
                    buf = w13[si % 2]
                    for dj in range(32):
                        sy.dma_start(out=buf[:, dj * 512: dj * 512 + nhj * 128],
                                     in_=w1Ts.ap()[dj * 128:(dj + 1) * 128,
                                                   hj0 * 128:(hj0 + nhj) * 128]).then_inc(S["wfl"], 16)
                        sy.dma_start(out=buf[:, dj * 512 + 256: dj * 512 + 256 + nhj * 128],
                                     in_=w3Ts.ap()[dj * 128:(dj + 1) * 128,
                                                   hj0 * 128:(hj0 + nhj) * 128]).then_inc(S["wfl"], 16)

            @block.gpsimd
            def _(g):
                for c in range(N_CH):
                    if c == 0:
                        g.wait_ge(S["dqst0"], NDQS0)
                        g.wait_ge(S["dqst1"], NDQS1)
                        g.wait_ge(S["sst0"], 16 * ((32 * NTC + 1) // 2))
                        g.wait_ge(S["sst1"], 16 * ((32 * NTC) // 2))
                        r_sh, t0 = 0, 0
                        for dj in range(32):
                            g.dma_start(out=p2xn[:, dj * CH_T:(dj + 1) * CH_T],
                                        in_=xn_v[r_sh, dj * 128:(dj + 1) * 128,
                                                 t0:t0 + CH_T]).then_inc(S["xl20"], 16)
                    for sw in range(NSW):
                        rec = p2s[c * NSW + sw]
                        si = c * NSW + sw
                        if si >= 1:
                            g.wait_ge(S["pey"], p2s[si - 1]["pey_1"])
                        for hj in range(rec["nhj"]):
                            g.dma_start(out=w2s[:, hj * D:(hj + 1) * D],
                                        in_=w2Ts.ap()[(rec["hj0"] + hj) * 128:
                                                      (rec["hj0"] + hj + 1) * 128, :]).then_inc(S["xl21"], 16)
                    if c + 1 < N_CH:
                        r_sh, t0 = divmod((c + 1) * CH_T, SHARD)
                        g.wait_ge(S["peh"], p2s[c * NSW + NSW - 1]["peh_1"])
                        for dj in range(32):
                            g.dma_start(out=p2xn[:, dj * CH_T:(dj + 1) * CH_T],
                                        in_=xn_v[r_sh, dj * 128:(dj + 1) * 128,
                                                 t0:t0 + CH_T]).then_inc(S["xl20"], 16)
                    g.wait_ge(S["yev"], p2s[c * NSW + NSW - 1]["yev_1"])
                    for grp in range(4):
                        dview = y_part.ap()[c, grp * 1024:(grp + 1) * 1024, :] \
                            .rearrange("(o p) t -> p o t", p=128)
                        g.dma_start(out=dview,
                                    in_=y_sb[:, grp * 8 * CH_T:(grp + 1) * 8 * CH_T]
                                    .rearrange("p (o t) -> p o t", t=CH_T)).then_inc(S["yst"], 16)
                    g.wait_ge(S["yst"], 16 * 4 * (c + 1))
                    g.collective_compute("ReduceScatter", OP.add,
                                         replica_groups=[list(range(NC_))],
                                         ins=[y_part.ap()[c, :, :].opt()],
                                         outs=[y_red.ap()[c, :, :].opt()]).then_inc(S["ccy"], 1)

            @block.tensor
            def _(t):
                for si, rec in enumerate(p2s):
                    c, nhj = rec["c"], rec["nhj"]
                    buf = w13[si % 2]
                    t.wait_ge(S["wfl"], rec["wfl_1"])
                    t.wait_ge(S["xl20"], 16 * 32 * (c + 1))
                    for hj in range(nhj):
                        gidx = rec["hgt_0"] + hj
                        pb1 = psA[2 * (gidx % 2)]
                        pb3 = psA[2 * (gidx % 2) + 1]
                        if gidx >= 2:
                            t.wait_ge(S["hgt"], gidx - 1)
                        for dj in range(32):
                            t.matmul(pb1[:, 0:CH_T],
                                     buf[:, dj * 512 + hj * 128: dj * 512 + (hj + 1) * 128],
                                     p2xn[:, dj * CH_T:(dj + 1) * CH_T],
                                     start=(dj == 0), stop=(dj == 31))
                            mi = t.matmul(pb3[:, 0:CH_T],
                                          buf[:, dj * 512 + 256 + hj * 128:
                                              dj * 512 + 256 + (hj + 1) * 128],
                                          p2xn[:, dj * CH_T:(dj + 1) * CH_T],
                                          start=(dj == 0), stop=(dj == 31))
                        mi.then_inc(S["peh"], 1)
                    t.wait_ge(S["hgt"], rec["hgt_1"])
                    t.wait_ge(S["xl21"], rec["w2l_1"])
                    for oj in range(32):
                        yidx = rec["pey_0"] + oj
                        pb = psA[4 + oj % 2]
                        if yidx >= 2:
                            t.wait_ge(S["yev"], yidx - 1)
                        for hj in range(nhj):
                            mi = t.matmul(pb[:, 0:CH_T],
                                          w2s[:, hj * D + oj * 128: hj * D + (oj + 1) * 128],
                                          htb[:, hj * CH_T:(hj + 1) * CH_T],
                                          start=(hj == 0), stop=(hj == nhj - 1))
                        mi.then_inc(S["pey"], 1)

            @block.scalar
            def _(s):
                for si, rec in enumerate(p2s):
                    c, sw, nhj = rec["c"], rec["sw"], rec["nhj"]
                    for hj in range(nhj):
                        s.wait_ge(S["peh"], rec["peh_0"] + hj + 1)
                        gidx = rec["hev_0"] + hj
                        if gidx >= 2:
                            s.wait_ge(S["hgt"], gidx - 1)
                        s.activation(hsb[gidx % 2][:, 0:CH_T], psA[2 * (gidx % 2)][:, 0:CH_T],
                                     AF.Silu).then_inc(S["hev"], 1)
                    if sw == 0:
                        if c >= 1:
                            s.wait_ge(S["yst"], 16 * 4 * c)
                        for oj in range(32):
                            s.wait_ge(S["pey"], rec["pey_0"] + oj + 1)
                            s.copy(y_sb[:, oj * CH_T:(oj + 1) * CH_T],
                                   psA[4 + oj % 2][:, 0:CH_T]).then_inc(S["yev"], 1)

            @block.vector
            def _(v):
                for si, rec in enumerate(p2s):
                    sw, nhj = rec["sw"], rec["nhj"]
                    for hj in range(nhj):
                        gidx = rec["hev_0"] + hj
                        if hj == 0 and si >= 1:
                            v.wait_ge(S["pey"], p2s[si - 1]["pey_1"])
                        v.wait_ge(S["hev"], rec["hev_0"] + hj + 1)
                        v.tensor_tensor(htb[:, hj * CH_T:(hj + 1) * CH_T],
                                        hsb[gidx % 2][:, 0:CH_T],
                                        psA[2 * (gidx % 2) + 1][:, 0:CH_T],
                                        OP.mult).then_inc(S["hgt"], 1)
                    if sw >= 1:
                        v.wait_ge(S["yev"], rec["yev_0"])
                        for oj in range(32):
                            v.wait_ge(S["pey"], rec["pey_0"] + oj + 1)
                            v.tensor_add(y_sb[:, oj * CH_T:(oj + 1) * CH_T],
                                         y_sb[:, oj * CH_T:(oj + 1) * CH_T],
                                         psA[4 + oj % 2][:, 0:CH_T]).then_inc(S["yev"], 1)

        with ExitStack() as sc5:
            E5 = sc5.enter_context
            c_s = E5(nc.sbuf_tensor("c_s", [128, 4 * CH_T], F32))
            c_y = E5(nc.sbuf_tensor("c_y", [128, 4 * CH_T], F32))
            c_g = E5(nc.sbuf_tensor("c_g", [128, 4 * CH_T], F32))
            c_x = E5(nc.sbuf_tensor("c_x", [128, 4 * CH_T], F32))
            c_o = E5(nc.sbuf_tensor("c_o", [128, 4 * CH_T], F32))

            @block.sync
            def _(sy):
                sy.wait_ge(S["yst"], 16 * 4 * N_CH)
                for c in range(N_CH):
                    sy.wait_ge(S["ccy"], c + 1)
                    sk = (c * CH_T) // 512
                    soff = (c * CH_T) % 512
                    sy.wait_ge(S["ccs"], sk + 1)
                    if c >= 1:
                        sy.wait_ge(S["fc"], 4 * c - 1)
                    for r4 in range(4):
                        sy.dma_start(out=c_y[:, r4 * CH_T:(r4 + 1) * CH_T],
                                     in_=y_red.ap()[c, r4 * 128:(r4 + 1) * 128, :]).then_inc(S["fl"], 16)
                        sy.dma_start(out=c_s[:, r4 * CH_T:(r4 + 1) * CH_T],
                                     in_=s_red.ap()[sk, r4 * 128:(r4 + 1) * 128,
                                                    soff:soff + CH_T]).then_inc(S["fl"], 16)
                        sy.dma_start(out=c_x[:, r4 * CH_T:(r4 + 1) * CH_T],
                                     in_=xT_own[r4 * 128:(r4 + 1) * 128,
                                                c * CH_T:(c + 1) * CH_T]).then_inc(S["fl"], 16)

            @block.scalar
            def _(s):
                for c in range(N_CH):
                    s.wait_ge(S["fl"], 16 * 12 * (c + 1))
                    if c >= 1:
                        s.wait_ge(S["fc"], 4 * (c - 1) + 2)
                    s.activation(c_g[:, :], c_y[:, :], AF.Silu).then_inc(S["fc"], 1)
                    s.wait_ge(S["fc"], 4 * c + 3)
                    for r4 in range(4):
                        s.dma_start(out=outp[r4 * 128:(r4 + 1) * 128,
                                             c * CH_T:(c + 1) * CH_T],
                                    in_=c_o[:, r4 * CH_T:(r4 + 1) * CH_T]).then_inc(S["fo"], 16)
                    s.sem_inc(S["fc"], 1)

            @block.vector
            def _(v):
                for c in range(N_CH):
                    if c >= 1:
                        v.wait_ge(S["fo"], 16 * 4 * c)
                    v.wait_ge(S["fc"], 4 * c + 1)
                    v.tensor_mul(c_o[:, :], c_s[:, :], c_g[:, :]).then_inc(S["fc"], 1)
                    v.wait_ge(S["fc"], 4 * c + 2)
                    v.tensor_add(c_o[:, :], c_o[:, :], c_x[:, :]).then_inc(S["fc"], 1)

            @block.gpsimd
            def _(g):
                g.wait_ge(S["fo"], 16 * 4 * N_CH)

    return nc


_NC_CACHE = {}


def _get_nc(T, NB):
    key = (T, NB)
    if key not in _NC_CACHE:
        _NC_CACHE[key] = build_nc(T, NB)
    return _NC_CACHE[key]


def _prep_inputs(x, wq, wk, wv, wo, c1, a1, c2, a2, c3, a3, T, NB):
    NTOK = NB * T
    SHARD = NTOK // NC_
    xf = np.ascontiguousarray(np.asarray(x, dtype=np.float32).reshape(NTOK, D))
    xT = np.ascontiguousarray(xf.T)
    pos = np.arange(T, dtype=np.float64)
    cs = np.stack([np.cos(pos), np.sin(pos)], axis=1).astype(np.float32)
    cols = np.arange(512)[None, :]
    rows = np.arange(128)[:, None]
    dm = np.stack([np.where(cols < (m * 128 + rows + 1), 0.0, -1e30).astype(np.float32)
                   for m in range(4)], axis=0)
    ident = np.eye(128, dtype=np.float32)
    wq, wk, wv, wo = (np.asarray(a, dtype=np.float32) for a in (wq, wk, wv, wo))
    c1, c2, c3 = (np.asarray(a, dtype=np.int32) for a in (c1, c2, c3))
    a1, a2, a3 = (np.asarray(a, dtype=np.float32) for a in (a1, a2, a3))
    HIDR = 11008 // NC_

    in_maps = []
    for c in range(NC_):
        jr = slice(512 * c, 512 * (c + 1))
        hr = slice(HIDR * c, HIDR * (c + 1))

        def padded_rows(codes, amax):
            cod = np.full((HID, D), 7, dtype=np.int32)
            cod[:HIDR] = codes[hr]
            am = np.zeros((HID, D // 64), dtype=np.float32)
            am[:HIDR] = amax[hr]
            amx = np.repeat(am, 64, axis=1)
            return (np.ascontiguousarray(cod.T), np.ascontiguousarray(amx.T))

        c1T_, a1T_ = padded_rows(c1, a1)
        c3T_, a3T_ = padded_rows(c3, a3)
        cod2 = np.full((D, HID), 7, dtype=np.int32)
        cod2[:, :HIDR] = c2[:, hr]
        am2x = np.zeros((D, HID), dtype=np.float32)
        am2x[:, :HIDR] = np.repeat(a2, 64, axis=1)[:, hr]
        im = {
            "x_sh": np.ascontiguousarray(xf[SHARD * c:SHARD * (c + 1)]),
            "xT_own": np.ascontiguousarray(xT[512 * c:512 * (c + 1), :]),
            "wqT": np.ascontiguousarray(wq[jr].T),
            "wkT": np.ascontiguousarray(wk[jr].T),
            "wvT": np.ascontiguousarray(wv[jr].T),
            "woT": np.ascontiguousarray(wo[:, jr].T),
            "c1T": c1T_, "a1T": a1T_, "c3T": c3T_, "a3T": a3T_,
            "c2T": np.ascontiguousarray(cod2.T),
            "a2T": np.ascontiguousarray(am2x.T),
            "cs_in": cs, "dmask": dm, "ident_in": ident,
        }
        in_maps.append(im)
    return in_maps


def kernel(L0_attn_x, L0_attn_wq, L0_attn_wk, L0_attn_wv, L0_attn_wo,
           L0_ffn_w1_codes, L0_ffn_w1_absmax, L0_ffn_w2_codes, L0_ffn_w2_absmax,
           L0_ffn_w3_codes, L0_ffn_w3_absmax):
    x = np.asarray(L0_attn_x, dtype=np.float32)
    NB, T, _ = x.shape
    NTOK = NB * T
    nc = _get_nc(T, NB)
    in_maps = _prep_inputs(x, L0_attn_wq, L0_attn_wk, L0_attn_wv, L0_attn_wo,
                           L0_ffn_w1_codes, L0_ffn_w1_absmax,
                           L0_ffn_w2_codes, L0_ffn_w2_absmax,
                           L0_ffn_w3_codes, L0_ffn_w3_absmax, T, NB)
    res = run_bass_kernel_spmd(nc, in_maps, core_ids=list(range(NC_)))
    outT = np.empty((D, NTOK), dtype=np.float32)
    for c in range(NC_):
        outT[512 * c:512 * (c + 1)] = res.results[c]["outp"]
    return np.ascontiguousarray(outT.T).reshape(NB, T, D)


# version 07
# speedup vs baseline: 1.0164x; 1.0164x over previous
import numpy as np
import ml_dtypes
from contextlib import ExitStack
import concourse.bass as bass
import concourse.mybir as mybir
from concourse.bass_utils import run_bass_kernel_spmd

F32 = mybir.dt.float32
BF16 = mybir.dt.bfloat16
I32 = mybir.dt.int32
AF = mybir.ActivationFunctionType
OP = mybir.AluOpType
AX = mybir.AxisListType

NF4 = [-1.0, -0.6961928009986877, -0.5250730514526367, -0.39491748809814453,
       -0.28444138169288635, -0.18477343022823334, -0.09105003625154495, 0.0,
       0.07958029955625534, 0.16093020141124725, 0.24611230194568634,
       0.33791524171829224, 0.44070982933044434, 0.5626170039176941,
       0.6989939212799072, 1.0]

NC_ = 8
D = 4096
JW = 512
HID = 1408
SWEEPS = [(0, 3), (3, 6), (6, 9), (9, 11)]
SCALE = 128.0 ** -0.5


def build_nc(T=4096, NB=2):
    NTOK = NB * T
    SHARD = NTOK // NC_
    CH_T = min(512, SHARD)
    N_CH = NTOK // CH_T
    TT_B = T // 128
    NP1 = NB * TT_B
    NQC = T // 512
    NHB = NB * 4
    NLT = SHARD // 128
    N_SC = NTOK // 512
    NTC = NB * (T // 512)

    nc = bass.Bass("TRN2", target_bir_lowering=False)
    x_sh = nc.dram_tensor("x_sh", [SHARD, D], F32, kind="ExternalInput").ap()
    xT_own = nc.dram_tensor("xT_own", [512, NTOK], F32, kind="ExternalInput").ap()
    wqT = nc.dram_tensor("wqT", [D, JW], F32, kind="ExternalInput").ap()
    wkT = nc.dram_tensor("wkT", [D, JW], F32, kind="ExternalInput").ap()
    wvT = nc.dram_tensor("wvT", [D, JW], F32, kind="ExternalInput").ap()
    woT = nc.dram_tensor("woT", [JW, D], F32, kind="ExternalInput").ap()
    c1T = nc.dram_tensor("c1T", [D, HID], I32, kind="ExternalInput").ap()
    c3T = nc.dram_tensor("c3T", [D, HID], I32, kind="ExternalInput").ap()
    c2T = nc.dram_tensor("c2T", [HID, D], I32, kind="ExternalInput").ap()
    a1T = nc.dram_tensor("a1T", [D, HID], F32, kind="ExternalInput").ap()
    a3T = nc.dram_tensor("a3T", [D, HID], F32, kind="ExternalInput").ap()
    a2T = nc.dram_tensor("a2T", [HID, D], F32, kind="ExternalInput").ap()
    cs_in = nc.dram_tensor("cs_in", [T, 2], F32, kind="ExternalInput").ap()
    dmask = nc.dram_tensor("dmask", [4, 128, 512], F32, kind="ExternalInput").ap()
    ident_in = nc.dram_tensor("ident_in", [128, 128], F32, kind="ExternalInput").ap()
    outp = nc.dram_tensor("outp", [512, NTOK], F32, kind="ExternalOutput").ap()

    cc_xn = nc.dram_tensor("cc_xn", [D, SHARD], BF16)
    xn_ag = nc.dram_tensor("xn_ag", [NC_ * D, SHARD], BF16)
    q_fm = nc.dram_tensor("q_fm", [NHB, 128, T], BF16)
    k_fm = nc.dram_tensor("k_fm", [NHB, 128, T], BF16)
    v_tm = nc.dram_tensor("v_tm", [NHB, T, 128], BF16)
    attn_d = nc.dram_tensor("attn_d", [NHB, 128, T], BF16)
    w1Ts = nc.dram_tensor("w1Ts", [D, HID], BF16)
    w3Ts = nc.dram_tensor("w3Ts", [D, HID], BF16)
    w2Ts = nc.dram_tensor("w2Ts", [HID, D], BF16)
    s_part = nc.dram_tensor("s_part", [NTC, D, 512], F32)
    y_part = nc.dram_tensor("y_part", [N_CH, D, CH_T], F32)
    s_red = nc.dram_tensor("s_red", [NTC, 512, 512], F32)
    y_red = nc.dram_tensor("y_red", [N_CH, 512, CH_T], F32)

    xn_v = xn_ag.ap().rearrange("(r d) t -> r d t", r=NC_)

    dq_jobs = []
    for ti in range(32):
        dq_jobs.append((c1T, a1T, w1Ts, ti * 128, 0, HID))
    for ti in range(32):
        dq_jobs.append((c3T, a3T, w3Ts, ti * 128, 0, HID))
    for ti in range(11):
        for cj, w in ((0, 1408), (1408, 1408), (2816, 1280)):
            dq_jobs.append((c2T, a2T, w2Ts, ti * 128, cj, w))
    n_dq = len(dq_jobs)
    v_order = list(range(n_dq))
    nv = n_dq

    with ExitStack() as top:
        E = top.enter_context
        ident = E(nc.sbuf_tensor("ident", [128, 128], BF16))
        maskt = E(nc.sbuf_tensor("maskt", [128, 4 * 512], F32))
        cst = E(nc.sbuf_tensor("cst", [128, 2 * TT_B], F32))
        sx = E(nc.sbuf_tensor("sx", [128, 1], F32))
        sx2 = E(nc.sbuf_tensor("sx2", [128, 1], F32))
        mean = E(nc.sbuf_tensor("mean", [128, 1], F32))
        m2 = E(nc.sbuf_tensor("m2", [128, 1], F32))
        msq = E(nc.sbuf_tensor("msq", [128, 1], F32))
        var = E(nc.sbuf_tensor("var", [128, 1], F32))
        std = E(nc.sbuf_tensor("std", [128, 1], F32))
        rstd = E(nc.sbuf_tensor("rstd", [128, 1], F32))
        nmx_c = E(nc.sbuf_tensor("nmx_c", [128, 4], F32))
        sums_c = E(nc.sbuf_tensor("sums_c", [128, 4], F32))
        rsum_c = E(nc.sbuf_tensor("rsum_c", [128, 4], F32))
        psA = [E(nc.psum_tensor(f"psA{i}", [128, 512], F32)) for i in range(6)]
        psT = [E(nc.psum_tensor(f"psT{i}", [128, 512], BF16)) for i in range(2)]
        psB = psA[5]

        sems = ["cl", "wl", "lx0", "lx1", "lns", "g2", "pes", "pt", "agw", "cc",
                "dql0", "dql1", "dqc2", "dqw", "dqst0", "dqst1", "dqg", "rpg",
                "xl10", "xl11", "pe1", "rp", "ptr", "aev", "aevs", "spv", "spq",
                "al0", "al1", "pe3", "sev", "sfm", "rcp", "rnm", "pvd",
                "ast", "wol", "a4l0", "a4l1", "pe4", "sst0", "sst1", "ccs",
                "wfl", "xl20", "xl21", "peh", "hev", "hgt", "pey", "yev", "yst", "ccy",
                "fl", "fc", "fo"]
        S = {n: E(nc.semaphore(n)) for n in sems}
        block = E(nc.Block())

        @block.gpsimd
        def _(g):
            g.dma_start(out=ident[:, :], in_=ident_in[:, :]).then_inc(S["wl"], 16)
            g.wait_ge(S["agw"], 16 * 32)
            g.collective_compute("AllGather", OP.bypass,
                                 replica_groups=[list(range(NC_))],
                                 ins=[cc_xn.ap().opt()],
                                 outs=[xn_ag.ap().opt()]).then_inc(S["cc"], 1)

        @block.sync
        def _(sy):
            for i in range(4):
                sy.dma_start(out=maskt[:, i * 512:(i + 1) * 512],
                             in_=dmask[i, :, :]).then_inc(S["cl"], 16)
            for i in range(TT_B):
                sy.dma_start(out=cst[:, 2 * i:2 * i + 2],
                             in_=cs_in[i * 128:(i + 1) * 128, :]).then_inc(S["cl"], 16)

        with ExitStack() as dsc:
            ED = dsc.enter_context
            dqc = [ED(nc.sbuf_tensor(f"dqc{i}", [128, 1408], I32)) for i in range(2)]
            dqb = ED(nc.sbuf_tensor("dqb", [128, 1408], BF16))
            dqsl = ED(nc.sbuf_tensor("dqsl", [128, 6 * 1408], BF16))
            dqm = [ED(nc.sbuf_tensor(f"dqm{i}", [128, 1408], F32)) for i in range(2)]
            dqo = [ED(nc.sbuf_tensor(f"dqo{i}", [128, 1408], BF16)) for i in range(2)]

            dq_li = [0]
            dq_ci = [0]
            dq_si = [0]
            dq_gc = [0]
            SELV = [(float(vv), NF4[vv]) for vv in range(16) if NF4[vv] != 0.0]

            def emit_dq_load(eng):
                i = dq_li[0]
                if i >= nv:
                    return
                dq_li[0] += 1
                src_c, src_a, dst, r0, c0, w = dq_jobs[v_order[i]]
                if i >= 2:
                    eng.wait_ge(S["dqc2"], i - 1)
                    eng.wait_ge(S["dqw"], i - 1)
                eng.dma_start(out=dqc[i % 2][:, 0:w],
                              in_=src_c[r0:r0 + 128, c0:c0 + w]).then_inc(S[f"dql{i % 2}"], 16)
                eng.dma_start(out=dqm[i % 2][:, 0:w],
                              in_=src_a[r0:r0 + 128, c0:c0 + w]).then_inc(S[f"dql{i % 2}"], 16)

            def emit_dq_comp(v):
                i = dq_ci[0]
                if i >= nv:
                    return
                dq_ci[0] += 1
                _, _, dst, r0, c0, w = dq_jobs[v_order[i]]
                v.wait_ge(S[f"dql{i % 2}"], 32 * (i // 2 + 1))
                if i >= 2:
                    v.wait_ge(S[f"dqst{i % 2}"], 16 * (i // 2))
                v.tensor_copy(dqb[:, 0:w], dqc[i % 2][:, 0:w]).then_inc(S["dqc2"], 1)
                v.wait_ge(S["dqc2"], i + 1)

                def sl(j):
                    return dqsl[:, j * 1408: j * 1408 + w]
                pend = list(SELV)
                filled = []
                free = [0, 1, 2, 3, 4, 5]
                while pend or len(filled) > 1:
                    nxt = []
                    adds = []
                    sels = []
                    while len(filled) >= 2:
                        a = filled.pop(0)
                        b = filled.pop(0)
                        adds.append((a, b))
                        nxt.append(a)
                        free.append(b)
                    while free and pend:
                        s_ = free.pop(0)
                        sels.append((s_, pend.pop(0)))
                        nxt.append(s_)
                    for a, b in adds:
                        v.tensor_add(sl(a), sl(a), sl(b)).then_inc(S["dqg"], 1)
                        dq_gc[0] += 1
                    for s_, (vv, tv) in sels:
                        v.tensor_scalar(sl(s_), dqb[:, 0:w], vv, tv,
                                        OP.is_equal, OP.mult).then_inc(S["dqg"], 1)
                        dq_gc[0] += 1
                    v.wait_ge(S["dqg"], dq_gc[0])
                    filled = nxt + filled
                v.tensor_mul(dqo[i % 2][:, 0:w], sl(filled[0]),
                             dqm[i % 2][:, 0:w]).then_inc(S["dqw"], 1)

            def emit_dq_store(eng):
                i = dq_si[0]
                if i >= nv:
                    return
                dq_si[0] += 1
                _, _, dst, r0, c0, w = dq_jobs[v_order[i]]
                eng.wait_ge(S["dqw"], i + 1)
                eng.dma_start(out=dst.ap()[r0:r0 + 128, c0:c0 + w],
                              in_=dqo[i % 2][:, 0:w]).then_inc(S[f"dqst{i % 2}"], 16)

            with ExitStack() as sc0:
                E0 = sc0.enter_context
                xt = [E0(nc.sbuf_tensor(f"xt{i}", [128, D], F32)) for i in range(2)]
                xnt = E0(nc.sbuf_tensor("xnt", [128, D], BF16))
                sqscr = E0(nc.sbuf_tensor("sqscr", [128, D], BF16))
                stage = E0(nc.sbuf_tensor("stage", [128, 32 * SHARD], BF16))

                @block.sync
                def _(sy):
                    for i in range(NLT):
                        if i >= 2:
                            sy.wait_ge(S["pt"], (i - 1) * 32)
                        sy.dma_start(out=xt[i % 2][:, :],
                                     in_=x_sh[i * 128:(i + 1) * 128, :]).then_inc(S[f"lx{i % 2}"], 16)
                        emit_dq_load(sy)

                @block.scalar
                def _(s):
                    for i in range(NLT):
                        s.wait_ge(S[f"lx{i % 2}"], 16 * (i // 2 + 1))
                        if i >= 1:
                            s.wait_ge(S["g2"], 7 * (i - 1) + 3)
                        s.activation(sqscr[:, :], xt[i % 2][:, :], AF.Square,
                                     accum_out=sx2[:, :]).then_inc(S["lns"], 1)
                        s.wait_ge(S["g2"], 7 * i + 5)
                        s.activation(std[:, :], var[:, :], AF.Sqrt).then_inc(S["lns"], 1)
                        emit_dq_store(s)

                @block.vector
                def _(v):
                    for i in range(NLT):
                        b7 = 7 * i
                        v.wait_ge(S[f"lx{i % 2}"], 16 * (i // 2 + 1))
                        v.reduce_sum(sx[:, :], xt[i % 2][:, :], AX.X).then_inc(S["g2"], 1)
                        v.wait_ge(S["lns"], 2 * i + 1)
                        v.wait_ge(S["g2"], b7 + 1)
                        v.tensor_scalar_mul(mean[:, :], sx[:, :], 1.0 / D).then_inc(S["g2"], 1)
                        v.wait_ge(S["g2"], b7 + 2)
                        v.tensor_scalar_mul(m2[:, :], sx2[:, :], 1.0 / D).then_inc(S["g2"], 1)
                        v.wait_ge(S["g2"], b7 + 3)
                        v.tensor_scalar_mul(msq[:, :], mean[:, :], mean[:, :]).then_inc(S["g2"], 1)
                        v.wait_ge(S["g2"], b7 + 4)
                        v.tensor_scalar(var[:, :], m2[:, :], msq[:, :], 1e-5,
                                        OP.subtract, OP.add).then_inc(S["g2"], 1)
                        v.wait_ge(S["lns"], 2 * i + 2)
                        v.reciprocal(rstd[:, :], std[:, :]).then_inc(S["g2"], 1)
                        v.wait_ge(S["g2"], b7 + 6)
                        if i >= 1:
                            v.wait_ge(S["pes"], 32 * i)
                        v.tensor_scalar(xnt[:, :], xt[i % 2][:, :], mean[:, :], rstd[:, :],
                                        OP.subtract, OP.mult).then_inc(S["g2"], 1)
                        for r in range(8):
                            v.wait_ge(S["pes"], i * 32 + (r + 1) * 4)
                            for k in range(4):
                                j = r * 4 + k
                                ci_ = v.tensor_copy(
                                    stage[:, j * SHARD + i * 128: j * SHARD + (i + 1) * 128],
                                    psT[r % 2][:, k * 128:(k + 1) * 128])
                            ci_.then_inc(S["pt"], 4)
                        emit_dq_comp(v)

                @block.tensor
                def _(t):
                    t.wait_ge(S["wl"], 16)
                    for i in range(NLT):
                        t.wait_ge(S["g2"], 7 * i + 7)
                        for r in range(8):
                            if i * 32 + r * 4 >= 8:
                                t.wait_ge(S["pt"], i * 32 + r * 4 - 4)
                            for k in range(4):
                                j = r * 4 + k
                                t.transpose(psT[r % 2][:, k * 128:(k + 1) * 128],
                                            xnt[:, j * 128:(j + 1) * 128],
                                            ident[:, :]).then_inc(S["pes"], 1)

                @block.scalar
                def _(s):
                    s.wait_ge(S["pt"], NLT * 32)
                    for j in range(32):
                        s.dma_start(out=cc_xn.ap()[j * 128:(j + 1) * 128, :],
                                    in_=stage[:, j * SHARD:(j + 1) * SHARD]).then_inc(S["agw"], 16)
                    for _ in range(dq_ci[0] - dq_si[0]):
                        emit_dq_store(s)

            with ExitStack() as sc1:
                E1 = sc1.enter_context
                wq_sb = E1(nc.sbuf_tensor("wq_sb", [128, 32 * JW], BF16))
                wk_sb = E1(nc.sbuf_tensor("wk_sb", [128, 32 * JW], BF16))
                wv_sb = E1(nc.sbuf_tensor("wv_sb", [128, 32 * JW], BF16))
                p1xn = [E1(nc.sbuf_tensor(f"p1x{i}", [128, 32 * 128], BF16)) for i in range(2)]
                ropet = E1(nc.sbuf_tensor("ropet", [128, 512], F32))
                qr_sb = E1(nc.sbuf_tensor("qr_sb", [128, JW], BF16))
                kr_sb = E1(nc.sbuf_tensor("kr_sb", [128, JW], BF16))
                vr_sb = E1(nc.sbuf_tensor("vr_sb", [128, JW], BF16))
                tstg = [E1(nc.sbuf_tensor(f"tstg{i}", [128, 512], BF16)) for i in range(2)]

                @block.gpsimd
                def _(g):
                    g.wait_ge(S["agw"], 16 * 32)
                    for j in range(32):
                        g.dma_start(out=wq_sb[:, j * JW:(j + 1) * JW],
                                    in_=wqT[j * 128:(j + 1) * 128, :]).then_inc(S["wl"], 16)
                        g.dma_start(out=wk_sb[:, j * JW:(j + 1) * JW],
                                    in_=wkT[j * 128:(j + 1) * 128, :]).then_inc(S["wl"], 16)
                        g.dma_start(out=wv_sb[:, j * JW:(j + 1) * JW],
                                    in_=wvT[j * 128:(j + 1) * 128, :]).then_inc(S["wl"], 16)

                @block.sync
                def _(sy):
                    for idx in range(NP1):
                        b, ti = divmod(idx, TT_B)
                        r_sh, t0 = divmod(b * T + ti * 128, SHARD)
                        if idx == 0:
                            sy.wait_ge(S["cc"], 1)
                        if idx >= 2:
                            sy.wait_ge(S["pe1"], 96 * (idx - 1))
                        for dj in range(32):
                            sy.dma_start(out=p1xn[idx % 2][:, dj * 128:(dj + 1) * 128],
                                         in_=xn_v[r_sh, dj * 128:(dj + 1) * 128,
                                                  t0:t0 + 128]).then_inc(S[f"xl1{idx % 2}"], 16)
                        emit_dq_load(sy)

                @block.tensor
                def _(t):
                    t.wait_ge(S["wl"], 16 * 97)
                    for idx in range(NP1):
                        t.wait_ge(S[f"xl1{idx % 2}"], 16 * 32 * (idx // 2 + 1))
                        if idx >= 2:
                            t.wait_ge(S["rp"], 3 * (idx - 1))
                        for pi, w_sb in enumerate((wq_sb, wk_sb, wv_sb)):
                            pb = psA[(idx % 2) * 3 + pi]
                            for dj in range(32):
                                t.matmul(pb[:, :], p1xn[idx % 2][:, dj * 128:(dj + 1) * 128],
                                         w_sb[:, dj * JW:(dj + 1) * JW],
                                         start=(dj == 0), stop=(dj == 31)).then_inc(S["pe1"], 1)
                        if idx >= 1:
                            pidx = idx - 1
                            t.wait_ge(S["rp"], 3 * (pidx + 1))
                            if pidx >= 1:
                                t.wait_ge(S["aev"], 8 * pidx)
                            for gi, src in enumerate((qr_sb, kr_sb)):
                                for h in range(4):
                                    t.transpose(psT[gi][:, h * 128:(h + 1) * 128],
                                                src[:, h * 128:(h + 1) * 128],
                                                ident[:, :]).then_inc(S["ptr"], 1)
                    t.wait_ge(S["rp"], 3 * NP1)
                    t.wait_ge(S["aev"], 8 * (NP1 - 1))
                    for gi, src in enumerate((qr_sb, kr_sb)):
                        for h in range(4):
                            t.transpose(psT[gi][:, h * 128:(h + 1) * 128],
                                        src[:, h * 128:(h + 1) * 128],
                                        ident[:, :]).then_inc(S["ptr"], 1)

                @block.vector
                def _(v):
                    v.wait_ge(S["cl"], 16 * (4 + TT_B))
                    for idx in range(NP1):
                        b, ti = divmod(idx, TT_B)
                        co = cst[:, 2 * ti:2 * ti + 1]
                        si = cst[:, 2 * ti + 1:2 * ti + 2]
                        v.wait_ge(S["pe1"], 96 * idx + 32)
                        if idx >= 1:
                            v.wait_ge(S["ptr"], 8 * idx)
                            v.wait_ge(S["spv"], 16 * 4 * idx)
                        pq = psA[(idx % 2) * 3 + 0]
                        pk = psA[(idx % 2) * 3 + 1]
                        pv = psA[(idx % 2) * 3 + 2]
                        for gi, (ps_, dst) in enumerate(((pq, qr_sb), (pk, kr_sb))):
                            if gi == 1:
                                v.wait_ge(S["pe1"], 96 * idx + 64)
                            base_r = 4 * (2 * idx + gi)
                            v.tensor_scalar_mul(ropet[:, 0:256], ps_[:, 1:JW:2], si).then_inc(S["rpg"], 1)
                            v.wait_ge(S["rpg"], base_r + 1)
                            v.scalar_tensor_tensor(dst[:, 0:JW:2], ps_[:, 0:JW:2], co,
                                                   ropet[:, 0:256], OP.mult, OP.subtract).then_inc(S["rpg"], 1)
                            v.tensor_scalar_mul(ropet[:, 256:512], ps_[:, 0:JW:2], si).then_inc(S["rpg"], 1)
                            v.wait_ge(S["rpg"], base_r + 3)
                            v.scalar_tensor_tensor(dst[:, 1:JW:2], ps_[:, 1:JW:2], co,
                                                   ropet[:, 256:512], OP.mult, OP.add).then_inc(S["rpg"], 1)
                        v.wait_ge(S["pe1"], 96 * (idx + 1))
                        v.tensor_copy(vr_sb[:, :], pv[:, :]).then_inc(S["rp"], 3)
                        emit_dq_comp(v)

                @block.scalar
                def _(s):
                    for idx in range(NP1):
                        b, ti = divmod(idx, TT_B)
                        s.wait_ge(S["rp"], 3 * (idx + 1))
                        for h in range(4):
                            s.dma_start(out=v_tm.ap()[b * 4 + h, ti * 128:(ti + 1) * 128, :],
                                        in_=vr_sb[:, h * 128:(h + 1) * 128]).then_inc(S["spv"], 16)
                        pidx = idx - 1
                        if pidx >= 0:
                            pb_, pti = divmod(pidx, TT_B)
                            s.wait_ge(S["ptr"], 8 * (pidx + 1))
                            if pidx >= 1:
                                s.wait_ge(S["spq"], 16 * 8 * pidx)
                            for gi, dstd in enumerate((q_fm, k_fm)):
                                s.copy(tstg[gi][:, :], psT[gi][:, :]).then_inc(S["aev"], 4)
                                s.wait_ge(S["aev"], 8 * pidx + 4 * (gi + 1))
                                for h in range(4):
                                    s.dma_start(
                                        out=dstd.ap()[pb_ * 4 + h, :, pti * 128:(pti + 1) * 128],
                                        in_=tstg[gi][:, h * 128:(h + 1) * 128]).then_inc(S["spq"], 16)
                        emit_dq_store(s)
                    pidx = NP1 - 1
                    pb_, pti = divmod(pidx, TT_B)
                    s.wait_ge(S["ptr"], 8 * NP1)
                    s.wait_ge(S["spq"], 16 * 8 * pidx)
                    for gi, dstd in enumerate((q_fm, k_fm)):
                        s.copy(tstg[gi][:, :], psT[gi][:, :]).then_inc(S["aev"], 4)
                        s.wait_ge(S["aev"], 8 * pidx + 4 * (gi + 1))
                        for h in range(4):
                            s.dma_start(out=dstd.ap()[pb_ * 4 + h, :, pti * 128:(pti + 1) * 128],
                                        in_=tstg[gi][:, h * 128:(h + 1) * 128]).then_inc(S["spq"], 16)

            NPTR1 = 8 * NP1
            NAEV1 = 8 * NP1

            pe3c = 0
            sevc = 0
            sfmc = 0
            rcpc = 0
            rnmc = 0
            ptrc = NPTR1
            aevvc = NAEV1
            aevsc = 0
            pvdc = 0
            alc = 0
            sched = []
            for hb in range(NHB):
                alc += 16 * 3 * (T // 512)
                for tqc in range(NQC):
                    nkc = tqc + 1
                    tiles = []
                    for tqt in range(4):
                        rec = dict(al=alc, pe3_0=pe3c, sev_0=sevc, sfm_0=sfmc,
                                   rcp_0=rcpc, rnm_0=rnmc)
                        pe3c += nkc
                        sevc += nkc + 1
                        sfmc += 1
                        rcpc += 1
                        rnmc += 1
                        rec.update(pe3_1=pe3c, sev_1=sevc, sfm_1=sfmc,
                                   rcp_1=rcpc, rnm_1=rnmc)
                        tiles.append(rec)
                    kts = []
                    for kt in range(4 * nkc):
                        krec = dict(ptr_0=ptrc, pe3_0=pe3c)
                        if kt % 2 == 0:
                            krec["cps"] = "aev"
                            krec["cp_0"] = aevvc
                            aevvc += 4
                            krec["cp_1"] = aevvc
                        else:
                            krec["cps"] = "aevs"
                            krec["cp_0"] = aevsc
                            aevsc += 4
                            krec["cp_1"] = aevsc
                        ptrc += 4
                        pe3c += 1
                        krec.update(ptr_1=ptrc, pe3_1=pe3c)
                        kts.append(krec)
                    pvdc += 1
                    sched.append(dict(hb=hb, tqc=tqc, nkc=nkc, tiles=tiles, kts=kts,
                                      pvd=pvdc))
            NPE3 = pe3c
            NPVD = 16 * NHB * NQC

            with ExitStack() as sc3:
                E3 = sc3.enter_context
                qf = E3(nc.sbuf_tensor("qf", [128, T], BF16))
                kf = E3(nc.sbuf_tensor("kf", [128, T], BF16))
                vf = E3(nc.sbuf_tensor("vf", [128, T], BF16))
                ssb = E3(nc.sbuf_tensor("ssb", [128, 2 * T], F32))
                probs = E3(nc.sbuf_tensor("probs", [128, 4 * T], BF16))
                probsT = [E3(nc.sbuf_tensor(f"pT{i}", [128, 512], BF16)) for i in range(2)]
                astg = E3(nc.sbuf_tensor("astg", [128, 512], BF16))

                @block.sync
                def _(sy):
                    for hb in range(NHB):
                        if hb == 0:
                            sy.wait_ge(S["spq"], 16 * 8 * NP1)
                            sy.wait_ge(S["spv"], 16 * 4 * NP1)
                        else:
                            sy.wait_ge(S["pvd"], 16 * hb * NQC)
                        for j in range(T // 512):
                            sy.dma_start(out=qf[:, j * 512:(j + 1) * 512],
                                         in_=q_fm.ap()[hb, :, j * 512:(j + 1) * 512]).then_inc(S[f"al{hb % 2}"], 16)
                            sy.dma_start(out=kf[:, j * 512:(j + 1) * 512],
                                         in_=k_fm.ap()[hb, :, j * 512:(j + 1) * 512]).then_inc(S[f"al{hb % 2}"], 16)
                            sy.dma_start(
                                out=vf[:, j * 512:(j + 1) * 512].rearrange(
                                    "p (a h) -> p a h", h=128),
                                in_=v_tm.ap()[hb, j * 512:(j + 1) * 512, :]
                                .rearrange("(a p) h -> p a h", p=128)).then_inc(S[f"al{hb % 2}"], 16)
                        for _ in range(NQC):
                            emit_dq_load(sy)

                @block.tensor
                def _(t):
                    first = True
                    for ch in sched:
                        nkc = ch["nkc"]
                        for tqt, rec in enumerate(ch["tiles"]):
                            hb_ = ch["hb"]
                            per_hb = 16 * 3 * (T // 512)
                            t.wait_ge(S[f"al{hb_ % 2}"], per_hb * (hb_ // 2 + 1))
                            if first:
                                t.wait_ge(S["rp"], 3 * NP1)
                                first = False
                            if rec["sev_0"] >= 1:
                                t.wait_ge(S["sev"], rec["sev_0"])
                            qi = ch["tqc"] * 4 + tqt
                            for kc in range(nkc):
                                if kc >= 3:
                                    t.wait_ge(S["sev"], rec["sev_0"] + kc - 2)
                                t.matmul(psA[kc % 3][:, :], qf[:, qi * 128:(qi + 1) * 128],
                                         kf[:, kc * 512:(kc + 1) * 512],
                                         start=True, stop=True).then_inc(S["pe3"], 1)
                        t.wait_ge(S["rnm"], ch["tiles"][3]["rnm_1"])
                        for kt, krec in enumerate(ch["kts"]):
                            if kt >= 2:
                                t.wait_ge(S[krec["cps"]], krec["cp_0"])
                            if kt == 0:
                                t.wait_ge(S["pvd"], 16 * (ch["pvd"] - 1))
                            for tqt in range(4):
                                t.transpose(psT[kt % 2][:, tqt * 128:(tqt + 1) * 128],
                                            probs[:, tqt * T + kt * 128: tqt * T + (kt + 1) * 128],
                                            ident[:, :]).then_inc(S["ptr"], 1)
                            t.wait_ge(S[krec["cps"]], krec["cp_1"])
                            t.matmul(psB[:, :], vf[:, kt * 128:(kt + 1) * 128],
                                     probsT[kt % 2][:, :],
                                     start=(kt == 0),
                                     stop=(kt == 4 * nkc - 1)).then_inc(S["pe3"], 1)

                @block.vector
                def _(v):
                    v.wait_ge(S["cl"], 16 * (4 + TT_B))
                    for ci, ch in enumerate(sched):
                        nkc = ch["nkc"]
                        for tqt, rec in enumerate(ch["tiles"]):
                            g = 4 * ci + tqt
                            hoff = (g % 2) * T
                            kc = nkc - 1
                            v.wait_ge(S["pe3"], rec["pe3_0"] + kc + 1)
                            if g >= 2:
                                v.wait_ge(S["sfm"], g - 1)
                            v.scalar_tensor_tensor(ssb[:, hoff + kc * 512: hoff + (kc + 1) * 512],
                                                   psA[kc % 3][:, :], SCALE,
                                                   maskt[:, tqt * 512:(tqt + 1) * 512],
                                                   OP.mult, OP.add).then_inc(S["sev"], 1)
                            if ci >= 1:
                                v.wait_ge(S["sfm"], 4 * (ci - 1) + tqt + 1)
                            v.wait_ge(S["sev"], rec["sev_0"] + nkc)
                            v.reduce_max(nmx_c[:, tqt:tqt + 1],
                                         ssb[:, hoff: hoff + nkc * 512], AX.X,
                                         negate=True).then_inc(S["sev"], 1)
                        for tqt, rec in enumerate(ch["tiles"]):
                            v.wait_ge(S["sfm"], rec["sfm_1"])
                            v.reciprocal(rsum_c[:, tqt:tqt + 1],
                                         sums_c[:, tqt:tqt + 1]).then_inc(S["rcp"], 1)
                        for tqt, rec in enumerate(ch["tiles"]):
                            v.wait_ge(S["rcp"], rec["rcp_1"])
                            v.tensor_scalar_mul(probs[:, tqt * T: tqt * T + nkc * 512],
                                                probs[:, tqt * T: tqt * T + nkc * 512],
                                                rsum_c[:, tqt:tqt + 1]).then_inc(S["rnm"], 1)
                        for kt, krec in enumerate(ch["kts"]):
                            if kt % 2 != 0:
                                continue
                            v.wait_ge(S["ptr"], krec["ptr_1"])
                            if kt >= 2:
                                v.wait_ge(S["pe3"], krec["pe3_0"] - 1)
                            v.tensor_copy(probsT[0][:, :],
                                          psT[0][:, :]).then_inc(S["aev"], 4)
                        emit_dq_comp(v)

                @block.scalar
                def _(s):
                    for ci, ch in enumerate(sched):
                        nkc = ch["nkc"]
                        if ci >= 1:
                            s.wait_ge(S["ptr"], sched[ci - 1]["kts"][-1]["ptr_1"])
                            s.wait_ge(S["rcp"], 4 * ci)
                        for tqt, rec in enumerate(ch["tiles"]):
                            g = 4 * ci + tqt
                            hoff = (g % 2) * T
                            for kc in range(nkc - 1):
                                s.wait_ge(S["pe3"], rec["pe3_0"] + kc + 1)
                                s.activation(ssb[:, hoff + kc * 512: hoff + (kc + 1) * 512],
                                             psA[kc % 3][:, :], AF.Copy,
                                             scale=SCALE).then_inc(S["sev"], 1)
                            s.wait_ge(S["sev"], rec["sev_1"])
                            s.activation(probs[:, tqt * T: tqt * T + nkc * 512],
                                         ssb[:, hoff: hoff + nkc * 512],
                                         AF.Exp, bias=nmx_c[:, tqt:tqt + 1],
                                         accum_out=sums_c[:, tqt:tqt + 1]).then_inc(S["sfm"], 1)
                        for kt, krec in enumerate(ch["kts"]):
                            if kt % 2 != 1:
                                continue
                            s.wait_ge(S["ptr"], krec["ptr_1"])
                            if kt >= 2:
                                s.wait_ge(S["pe3"], krec["pe3_0"] - 1)
                            s.copy(probsT[1][:, :], psT[1][:, :]).then_inc(S["aevs"], 4)
                        s.wait_ge(S["pe3"], ch["kts"][-1]["pe3_1"])
                        s.wait_ge(S["pvd"], 16 * (ch["pvd"] - 1))
                        s.copy(astg[:, :], psB[:, :]).then_inc(S["ast"], 1)
                        s.wait_ge(S["ast"], ch["pvd"])
                        s.dma_start(out=attn_d.ap()[ch["hb"], :,
                                                    ch["tqc"] * 512:(ch["tqc"] + 1) * 512],
                                    in_=astg[:, :]).then_inc(S["pvd"], 16)
                        emit_dq_store(s)

            with ExitStack() as sc4:
                E4 = sc4.enter_context
                wo_sb = E4(nc.sbuf_tensor("wo_sb", [128, 4 * D], BF16))
                a4 = E4(nc.sbuf_tensor("a4", [128, 8 * 512], BF16))
                sstg = [E4(nc.sbuf_tensor(f"sstg{i}", [128, 512], F32)) for i in range(2)]

                @block.gpsimd
                def _(g):
                    g.wait_ge(S["pvd"], NPVD)
                    for j in range(4):
                        g.dma_start(out=wo_sb[:, j * D:(j + 1) * D],
                                    in_=woT[j * 128:(j + 1) * 128, :]).then_inc(S["wol"], 16)

                @block.sync
                def _(sy):
                    for tcg in range(NTC):
                        b, tc = divmod(tcg, T // 512)
                        if tcg == 0:
                            sy.wait_ge(S["pvd"], NPVD)
                        if tcg >= 2:
                            sy.wait_ge(S["pe4"], (tcg - 1) * 32)
                        for h in range(4):
                            sy.dma_start(out=a4[:, ((tcg % 2) * 4 + h) * 512:((tcg % 2) * 4 + h + 1) * 512],
                                         in_=attn_d.ap()[b * 4 + h, :,
                                                         tc * 512:(tc + 1) * 512]).then_inc(S[f"a4l{tcg % 2}"], 16)
                        emit_dq_load(sy)

                @block.tensor
                def _(t):
                    t.wait_ge(S["wol"], 16 * 4)
                    for tcg in range(NTC):
                        t.wait_ge(S[f"a4l{tcg % 2}"], 16 * 4 * (tcg // 2 + 1))
                        for oj in range(32):
                            idx = tcg * 32 + oj
                            if idx >= 2:
                                t.wait_ge(S["pe4"], idx - 1)
                            for h in range(4):
                                t.matmul(psA[idx % 2][:, :],
                                         wo_sb[:, h * D + oj * 128: h * D + (oj + 1) * 128],
                                         a4[:, ((tcg % 2) * 4 + h) * 512:((tcg % 2) * 4 + h + 1) * 512],
                                         start=(h == 0), stop=(h == 3)).then_inc(S["pe3"], 1)

                @block.scalar
                def _(s):
                    for tcg in range(NTC):
                        for oj in range(32):
                            idx = tcg * 32 + oj
                            s.wait_ge(S["pe3"], NPE3 + 4 * (idx + 1))
                            if idx >= 2:
                                s.wait_ge(S[f"sst{idx % 2}"], 16 * (idx // 2))
                            s.copy(sstg[idx % 2][:, :], psA[idx % 2][:, :]).then_inc(S["pe4"], 1)
                            s.wait_ge(S["pe4"], idx + 1)
                            s.dma_start(out=s_part.ap()[tcg, oj * 128:(oj + 1) * 128, :],
                                        in_=sstg[idx % 2][:, :]).then_inc(S[f"sst{idx % 2}"], 16)
                        emit_dq_store(s)

                @block.vector
                def _(v):
                    for tcg in range(NTC):
                        emit_dq_comp(v)
                        emit_dq_comp(v)

                @block.gpsimd
                def _(g):
                    for k in range(NTC):
                        n_st = 32 * (k + 1)
                        g.wait_ge(S["sst0"], 16 * ((n_st + 1) // 2))
                        g.wait_ge(S["sst1"], 16 * (n_st // 2))
                        g.collective_compute("ReduceScatter", OP.add,
                                             replica_groups=[list(range(NC_))],
                                             ins=[s_part.ap()[k, :, :].opt()],
                                             outs=[s_red.ap()[k, :, :].opt()]).then_inc(S["ccs"], 1)

            while dq_ci[0] < nv or dq_li[0] < nv:
                @block.sync
                def _(sy):
                    emit_dq_load(sy)
                    if dq_li[0] < min(nv, dq_ci[0] + 2):
                        emit_dq_load(sy)

                @block.vector
                def _(v):
                    emit_dq_comp(v)

                @block.scalar
                def _(s):
                    emit_dq_store(s)
            while dq_si[0] < nv:
                @block.scalar
                def _(s):
                    emit_dq_store(s)

            NDQS0 = 16 * ((nv + 1) // 2)
            NDQS1 = 16 * (nv // 2)

        SW2 = [(0, 2), (2, 4), (4, 6), (6, 8), (8, 10), (10, 11)]
        NSW = len(SW2)
        with ExitStack() as sc2:
            E2 = sc2.enter_context
            w13 = [E2(nc.sbuf_tensor(f"w13_{i}", [128, 32 * 512], BF16)) for i in range(2)]
            w2s = E2(nc.sbuf_tensor("w2s", [128, 2 * D], BF16))
            p2xn = E2(nc.sbuf_tensor("p2xn", [128, 32 * CH_T], BF16))
            hsb = [E2(nc.sbuf_tensor(f"hsb{i}", [128, CH_T], BF16)) for i in range(2)]
            htb = E2(nc.sbuf_tensor("htb", [128, 2 * CH_T], BF16))
            y_sb = E2(nc.sbuf_tensor("y_sb", [128, 32 * CH_T], F32))

            p2s = []
            wflc = 0
            pehc = 0
            hevc = 0
            hgtc = 0
            peyc = 0
            yevc = 0
            w2lc = 0
            for c in range(N_CH):
                for sw, (hj0, hj1) in enumerate(SW2):
                    nhj = hj1 - hj0
                    rec = dict(c=c, sw=sw, hj0=hj0, nhj=nhj,
                               wfl_0=wflc, peh_0=pehc, hev_0=hevc, hgt_0=hgtc,
                               pey_0=peyc, yev_0=yevc, w2l_0=w2lc)
                    wflc += 16 * 2 * 32
                    pehc += nhj
                    hevc += nhj
                    hgtc += nhj
                    peyc += 32
                    yevc += 32
                    w2lc += 16 * nhj
                    rec.update(wfl_1=wflc, peh_1=pehc, hev_1=hevc, hgt_1=hgtc,
                               pey_1=peyc, yev_1=yevc, w2l_1=w2lc)
                    p2s.append(rec)

            @block.sync
            def _(sy):
                for si, rec in enumerate(p2s):
                    hj0, nhj = rec["hj0"], rec["nhj"]
                    if si == 0:
                        sy.wait_ge(S["dqst0"], NDQS0)
                        sy.wait_ge(S["dqst1"], NDQS1)
                        sy.wait_ge(S["sst0"], 16 * ((32 * NTC + 1) // 2))
                        sy.wait_ge(S["sst1"], 16 * ((32 * NTC) // 2))
                    if si >= 2:
                        sy.wait_ge(S["peh"], p2s[si - 2]["peh_1"])
                    buf = w13[si % 2]
                    for dj in range(32):
                        sy.dma_start(out=buf[:, dj * 512: dj * 512 + nhj * 128],
                                     in_=w1Ts.ap()[dj * 128:(dj + 1) * 128,
                                                   hj0 * 128:(hj0 + nhj) * 128]).then_inc(S["wfl"], 16)
                        sy.dma_start(out=buf[:, dj * 512 + 256: dj * 512 + 256 + nhj * 128],
                                     in_=w3Ts.ap()[dj * 128:(dj + 1) * 128,
                                                   hj0 * 128:(hj0 + nhj) * 128]).then_inc(S["wfl"], 16)

            @block.gpsimd
            def _(g):
                for c in range(N_CH):
                    if c == 0:
                        g.wait_ge(S["dqst0"], NDQS0)
                        g.wait_ge(S["dqst1"], NDQS1)
                        g.wait_ge(S["sst0"], 16 * ((32 * NTC + 1) // 2))
                        g.wait_ge(S["sst1"], 16 * ((32 * NTC) // 2))
                        r_sh, t0 = 0, 0
                        for dj in range(32):
                            g.dma_start(out=p2xn[:, dj * CH_T:(dj + 1) * CH_T],
                                        in_=xn_v[r_sh, dj * 128:(dj + 1) * 128,
                                                 t0:t0 + CH_T]).then_inc(S["xl20"], 16)
                    for sw in range(NSW):
                        rec = p2s[c * NSW + sw]
                        si = c * NSW + sw
                        if si >= 1:
                            g.wait_ge(S["pey"], p2s[si - 1]["pey_1"])
                        for hj in range(rec["nhj"]):
                            g.dma_start(out=w2s[:, hj * D:(hj + 1) * D],
                                        in_=w2Ts.ap()[(rec["hj0"] + hj) * 128:
                                                      (rec["hj0"] + hj + 1) * 128, :]).then_inc(S["xl21"], 16)
                    if c + 1 < N_CH:
                        r_sh, t0 = divmod((c + 1) * CH_T, SHARD)
                        g.wait_ge(S["peh"], p2s[c * NSW + NSW - 1]["peh_1"])
                        for dj in range(32):
                            g.dma_start(out=p2xn[:, dj * CH_T:(dj + 1) * CH_T],
                                        in_=xn_v[r_sh, dj * 128:(dj + 1) * 128,
                                                 t0:t0 + CH_T]).then_inc(S["xl20"], 16)
                    g.wait_ge(S["yev"], p2s[c * NSW + NSW - 1]["yev_1"])
                    for grp in range(4):
                        dview = y_part.ap()[c, grp * 1024:(grp + 1) * 1024, :] \
                            .rearrange("(o p) t -> p o t", p=128)
                        g.dma_start(out=dview,
                                    in_=y_sb[:, grp * 8 * CH_T:(grp + 1) * 8 * CH_T]
                                    .rearrange("p (o t) -> p o t", t=CH_T)).then_inc(S["yst"], 16)
                    g.wait_ge(S["yst"], 16 * 4 * (c + 1))
                    g.collective_compute("ReduceScatter", OP.add,
                                         replica_groups=[list(range(NC_))],
                                         ins=[y_part.ap()[c, :, :].opt()],
                                         outs=[y_red.ap()[c, :, :].opt()]).then_inc(S["ccy"], 1)

            @block.tensor
            def _(t):
                for si, rec in enumerate(p2s):
                    c, nhj = rec["c"], rec["nhj"]
                    buf = w13[si % 2]
                    t.wait_ge(S["wfl"], rec["wfl_1"])
                    t.wait_ge(S["xl20"], 16 * 32 * (c + 1))
                    for hj in range(nhj):
                        gidx = rec["hgt_0"] + hj
                        pb1 = psA[2 * (gidx % 2)]
                        pb3 = psA[2 * (gidx % 2) + 1]
                        if gidx >= 2:
                            t.wait_ge(S["hgt"], gidx - 1)
                        for dj in range(32):
                            t.matmul(pb1[:, 0:CH_T],
                                     buf[:, dj * 512 + hj * 128: dj * 512 + (hj + 1) * 128],
                                     p2xn[:, dj * CH_T:(dj + 1) * CH_T],
                                     start=(dj == 0), stop=(dj == 31))
                            mi = t.matmul(pb3[:, 0:CH_T],
                                          buf[:, dj * 512 + 256 + hj * 128:
                                              dj * 512 + 256 + (hj + 1) * 128],
                                          p2xn[:, dj * CH_T:(dj + 1) * CH_T],
                                          start=(dj == 0), stop=(dj == 31))
                        mi.then_inc(S["peh"], 1)
                    t.wait_ge(S["hgt"], rec["hgt_1"])
                    t.wait_ge(S["xl21"], rec["w2l_1"])
                    for oj in range(32):
                        yidx = rec["pey_0"] + oj
                        pb = psA[4 + oj % 2]
                        if yidx >= 2:
                            t.wait_ge(S["yev"], yidx - 1)
                        for hj in range(nhj):
                            mi = t.matmul(pb[:, 0:CH_T],
                                          w2s[:, hj * D + oj * 128: hj * D + (oj + 1) * 128],
                                          htb[:, hj * CH_T:(hj + 1) * CH_T],
                                          start=(hj == 0), stop=(hj == nhj - 1))
                        mi.then_inc(S["pey"], 1)

            @block.scalar
            def _(s):
                for si, rec in enumerate(p2s):
                    c, sw, nhj = rec["c"], rec["sw"], rec["nhj"]
                    for hj in range(nhj):
                        s.wait_ge(S["peh"], rec["peh_0"] + hj + 1)
                        gidx = rec["hev_0"] + hj
                        if gidx >= 2:
                            s.wait_ge(S["hgt"], gidx - 1)
                        s.activation(hsb[gidx % 2][:, 0:CH_T], psA[2 * (gidx % 2)][:, 0:CH_T],
                                     AF.Silu).then_inc(S["hev"], 1)
                    if sw == 0:
                        if c >= 1:
                            s.wait_ge(S["yst"], 16 * 4 * c)
                        for oj in range(32):
                            s.wait_ge(S["pey"], rec["pey_0"] + oj + 1)
                            s.copy(y_sb[:, oj * CH_T:(oj + 1) * CH_T],
                                   psA[4 + oj % 2][:, 0:CH_T]).then_inc(S["yev"], 1)

            @block.vector
            def _(v):
                for si, rec in enumerate(p2s):
                    sw, nhj = rec["sw"], rec["nhj"]
                    for hj in range(nhj):
                        gidx = rec["hev_0"] + hj
                        if hj == 0 and si >= 1:
                            v.wait_ge(S["pey"], p2s[si - 1]["pey_1"])
                        v.wait_ge(S["hev"], rec["hev_0"] + hj + 1)
                        v.tensor_tensor(htb[:, hj * CH_T:(hj + 1) * CH_T],
                                        hsb[gidx % 2][:, 0:CH_T],
                                        psA[2 * (gidx % 2) + 1][:, 0:CH_T],
                                        OP.mult).then_inc(S["hgt"], 1)
                    if sw >= 1:
                        v.wait_ge(S["yev"], rec["yev_0"])
                        for oj in range(32):
                            v.wait_ge(S["pey"], rec["pey_0"] + oj + 1)
                            v.tensor_add(y_sb[:, oj * CH_T:(oj + 1) * CH_T],
                                         y_sb[:, oj * CH_T:(oj + 1) * CH_T],
                                         psA[4 + oj % 2][:, 0:CH_T]).then_inc(S["yev"], 1)

        with ExitStack() as sc5:
            E5 = sc5.enter_context
            c_s = E5(nc.sbuf_tensor("c_s", [128, 4 * CH_T], F32))
            c_y = E5(nc.sbuf_tensor("c_y", [128, 4 * CH_T], F32))
            c_g = E5(nc.sbuf_tensor("c_g", [128, 4 * CH_T], F32))
            c_x = E5(nc.sbuf_tensor("c_x", [128, 4 * CH_T], F32))
            c_o = E5(nc.sbuf_tensor("c_o", [128, 4 * CH_T], F32))

            @block.sync
            def _(sy):
                sy.wait_ge(S["yst"], 16 * 4 * N_CH)
                for c in range(N_CH):
                    sy.wait_ge(S["ccy"], c + 1)
                    sk = (c * CH_T) // 512
                    soff = (c * CH_T) % 512
                    sy.wait_ge(S["ccs"], sk + 1)
                    if c >= 1:
                        sy.wait_ge(S["fc"], 4 * c - 1)
                    for r4 in range(4):
                        sy.dma_start(out=c_y[:, r4 * CH_T:(r4 + 1) * CH_T],
                                     in_=y_red.ap()[c, r4 * 128:(r4 + 1) * 128, :]).then_inc(S["fl"], 16)
                        sy.dma_start(out=c_s[:, r4 * CH_T:(r4 + 1) * CH_T],
                                     in_=s_red.ap()[sk, r4 * 128:(r4 + 1) * 128,
                                                    soff:soff + CH_T]).then_inc(S["fl"], 16)
                        sy.dma_start(out=c_x[:, r4 * CH_T:(r4 + 1) * CH_T],
                                     in_=xT_own[r4 * 128:(r4 + 1) * 128,
                                                c * CH_T:(c + 1) * CH_T]).then_inc(S["fl"], 16)

            @block.scalar
            def _(s):
                for c in range(N_CH):
                    s.wait_ge(S["fl"], 16 * 12 * (c + 1))
                    if c >= 1:
                        s.wait_ge(S["fc"], 4 * (c - 1) + 2)
                    s.activation(c_g[:, :], c_y[:, :], AF.Silu).then_inc(S["fc"], 1)
                    s.wait_ge(S["fc"], 4 * c + 3)
                    for r4 in range(4):
                        s.dma_start(out=outp[r4 * 128:(r4 + 1) * 128,
                                             c * CH_T:(c + 1) * CH_T],
                                    in_=c_o[:, r4 * CH_T:(r4 + 1) * CH_T]).then_inc(S["fo"], 16)
                    s.sem_inc(S["fc"], 1)

            @block.vector
            def _(v):
                for c in range(N_CH):
                    if c >= 1:
                        v.wait_ge(S["fo"], 16 * 4 * c)
                    v.wait_ge(S["fc"], 4 * c + 1)
                    v.tensor_mul(c_o[:, :], c_s[:, :], c_g[:, :]).then_inc(S["fc"], 1)
                    v.wait_ge(S["fc"], 4 * c + 2)
                    v.tensor_add(c_o[:, :], c_o[:, :], c_x[:, :]).then_inc(S["fc"], 1)

            @block.gpsimd
            def _(g):
                g.wait_ge(S["fo"], 16 * 4 * N_CH)

    return nc


_NC_CACHE = {}


def _get_nc(T, NB):
    key = (T, NB)
    if key not in _NC_CACHE:
        _NC_CACHE[key] = build_nc(T, NB)
    return _NC_CACHE[key]


def _prep_inputs(x, wq, wk, wv, wo, c1, a1, c2, a2, c3, a3, T, NB):
    NTOK = NB * T
    SHARD = NTOK // NC_
    xf = np.ascontiguousarray(np.asarray(x, dtype=np.float32).reshape(NTOK, D))
    xT = np.ascontiguousarray(xf.T)
    pos = np.arange(T, dtype=np.float64)
    cs = np.stack([np.cos(pos), np.sin(pos)], axis=1).astype(np.float32)
    cols = np.arange(512)[None, :]
    rows = np.arange(128)[:, None]
    dm = np.stack([np.where(cols < (m * 128 + rows + 1), 0.0, -1e30).astype(np.float32)
                   for m in range(4)], axis=0)
    ident = np.eye(128, dtype=np.float32)
    wq, wk, wv, wo = (np.asarray(a, dtype=np.float32) for a in (wq, wk, wv, wo))
    c1, c2, c3 = (np.asarray(a, dtype=np.int32) for a in (c1, c2, c3))
    a1, a2, a3 = (np.asarray(a, dtype=np.float32) for a in (a1, a2, a3))
    HIDR = 11008 // NC_

    in_maps = []
    for c in range(NC_):
        jr = slice(512 * c, 512 * (c + 1))
        hr = slice(HIDR * c, HIDR * (c + 1))

        def padded_rows(codes, amax):
            cod = np.full((HID, D), 7, dtype=np.int32)
            cod[:HIDR] = codes[hr]
            am = np.zeros((HID, D // 64), dtype=np.float32)
            am[:HIDR] = amax[hr]
            amx = np.repeat(am, 64, axis=1)
            return (np.ascontiguousarray(cod.T), np.ascontiguousarray(amx.T))

        c1T_, a1T_ = padded_rows(c1, a1)
        c3T_, a3T_ = padded_rows(c3, a3)
        cod2 = np.full((D, HID), 7, dtype=np.int32)
        cod2[:, :HIDR] = c2[:, hr]
        am2x = np.zeros((D, HID), dtype=np.float32)
        am2x[:, :HIDR] = np.repeat(a2, 64, axis=1)[:, hr]
        im = {
            "x_sh": np.ascontiguousarray(xf[SHARD * c:SHARD * (c + 1)]),
            "xT_own": np.ascontiguousarray(xT[512 * c:512 * (c + 1), :]),
            "wqT": np.ascontiguousarray(wq[jr].T),
            "wkT": np.ascontiguousarray(wk[jr].T),
            "wvT": np.ascontiguousarray(wv[jr].T),
            "woT": np.ascontiguousarray(wo[:, jr].T),
            "c1T": c1T_, "a1T": a1T_, "c3T": c3T_, "a3T": a3T_,
            "c2T": np.ascontiguousarray(cod2.T),
            "a2T": np.ascontiguousarray(am2x.T),
            "cs_in": cs, "dmask": dm, "ident_in": ident,
        }
        in_maps.append(im)
    return in_maps


def kernel(L0_attn_x, L0_attn_wq, L0_attn_wk, L0_attn_wv, L0_attn_wo,
           L0_ffn_w1_codes, L0_ffn_w1_absmax, L0_ffn_w2_codes, L0_ffn_w2_absmax,
           L0_ffn_w3_codes, L0_ffn_w3_absmax):
    x = np.asarray(L0_attn_x, dtype=np.float32)
    NB, T, _ = x.shape
    NTOK = NB * T
    nc = _get_nc(T, NB)
    in_maps = _prep_inputs(x, L0_attn_wq, L0_attn_wk, L0_attn_wv, L0_attn_wo,
                           L0_ffn_w1_codes, L0_ffn_w1_absmax,
                           L0_ffn_w2_codes, L0_ffn_w2_absmax,
                           L0_ffn_w3_codes, L0_ffn_w3_absmax, T, NB)
    res = run_bass_kernel_spmd(nc, in_maps, core_ids=list(range(NC_)))
    outT = np.empty((D, NTOK), dtype=np.float32)
    for c in range(NC_):
        outT[512 * c:512 * (c + 1)] = res.results[c]["outp"]
    return np.ascontiguousarray(outT.T).reshape(NB, T, D)
